# version 03
# baseline (speedup 1.0000x reference)
import math
from contextlib import ExitStack

import numpy as np
import ml_dtypes

import concourse.bass as bass
import concourse.mybir as mybir
from concourse.bass_utils import run_bass_kernel_spmd

F32 = mybir.dt.float32
BF16 = mybir.dt.bfloat16
AF = mybir.ActivationFunctionType
ALU = mybir.AluOpType
AX = mybir.AxisListType

B, S, D = 2, 4096, 2048
NH, HD = 16, 128
NCORE = 8
NSLOT = 4608
NKT = 36
OWN = 1024
NDT = 16
NEXP, FEXP, TOPK, FSH = 32, 768, 6, 1536
SCALE = 1.0 / math.sqrt(HD)
NEG = -9.0e15
EPS = 1e-5
GRP = 1536
NGRP = 3
MOE_GROUPS_OVERRIDE = None

ENGINES = ("tensor", "vector", "scalar", "sync", "gpsimd")


class Sched:

    def __init__(self):
        self.steps = {e: [] for e in ENGINES}
        self.cnt = {s: 0 for s in ("dG", "dS", "tS", "vS", "sS", "ph")}

    def wait(self, eng, sem, val):
        if val and val > 0:
            self.steps[eng].append(("wait", sem, val))

    def op(self, eng, fn, inc=None, n=1):
        v = None
        if inc is not None:
            self.cnt[inc] += n
            v = self.cnt[inc]
        self.steps[eng].append(("op", fn, inc, n))
        return v

    def barrier(self):
        self.wait("gpsimd", "dG", self.cnt["dG"])
        self.wait("sync", "dS", self.cnt["dS"])
        self.cnt["ph"] += len(ENGINES)
        tgt = self.cnt["ph"]
        for e in ENGINES:
            self.steps[e].append(("op", (lambda x: x.drain()), None, 1))
            self.steps[e].append(("op", (lambda x: x.nop()), "phx", 1))
            self.steps[e].append(("wait", "ph", tgt))
        return tgt


def _ln(nc, sch, resid, s2, ones_cb, ones_rf, resid_final_v, write_bf, s2_guard_t=0, pfx="ln"):
    with ExitStack() as ps:
        sbl = lambda name, shape, dt_: ps.enter_context(nc.sbuf_tensor(name, shape, dt_))
        stats = sbl(f"{pfx}_stats", [1, 5 * 1024], F32)
        bfs = [sbl(f"{pfx}_bfs{i}", [128, 1024], BF16) for i in range(2)]
        sqs = [sbl(f"{pfx}_sqs{i}", [128, 1024], BF16) for i in range(2)]
        t1f = sbl(f"{pfx}_t1f", [128, 512], F32)
        psStat = ps.enter_context(nc.psum_tensor(f"{pfx}_psStat", [1, 2048], F32))
        psA = ps.enter_context(nc.psum_tensor(f"{pfx}_psA", [128, 1024], F32))
        psB = ps.enter_context(nc.psum_tensor(f"{pfx}_psB", [128, 1024], F32))

        m_row = stats[:, 0:1024]
        s2m = stats[:, 1024:2048]
        a_row = stats[:, 2048:3072]
        b_row = stats[:, 3072:4096]
        sd_row = stats[:, 4096:5120]

        bfs_free = [0, 0]
        sqs_free = [0, 0]
        mm = 0
        for dt in range(NDT):
            par = dt % 2
            sch.wait("vector", "tS", bfs_free[par])
            if dt == 0:
                sch.wait("vector", "vS", resid_final_v)
            cv = sch.op(
                "vector",
                lambda e, dt=dt, par=par: e.tensor_copy(
                    bfs[par][:], resid[:, dt * 1024:(dt + 1) * 1024]),
                "vS", 1)
            sch.wait("scalar", "tS", sqs_free[par])
            sch.wait("scalar", "vS", resid_final_v)
            sv = sch.op(
                "scalar",
                lambda e, dt=dt, par=par: e.activation(
                    sqs[par][:], resid[:, dt * 1024:(dt + 1) * 1024], AF.Square),
                "sS", 1)
            sch.wait("tensor", "vS", cv)
            sch.wait("tensor", "sS", sv)
            for tc in range(2):
                sch.op(
                    "tensor",
                    lambda e, par=par, tc=tc, dt=dt: e.matmul(
                        psStat[0:1, tc * 512: tc * 512 + 512],
                        ones_cb[:], bfs[par][:, tc * 512: tc * 512 + 512],
                        start=(dt == 0), stop=(dt == NDT - 1)),
                    None)
                mm = sch.op(
                    "tensor",
                    lambda e, par=par, tc=tc, dt=dt: e.matmul(
                        psStat[0:1, 1024 + tc * 512: 1024 + tc * 512 + 512],
                        ones_cb[:], sqs[par][:, tc * 512: tc * 512 + 512],
                        start=(dt == 0), stop=(dt == NDT - 1)),
                    "tS", 1)
            bfs_free[par] = mm
            sqs_free[par] = mm

        def vstep(fn):
            v = sch.op("vector", fn, "vS", 1)
            sch.wait("vector", "vS", v)
            return v

        sch.wait("vector", "tS", mm)
        vstep(lambda e: e.tensor_scalar(m_row, psStat[0:1, 0:1024], 1.0 / D, None, ALU.mult))
        vstep(lambda e: e.tensor_scalar(s2m, psStat[0:1, 1024:2048], 1.0 / D, None, ALU.mult))
        vstep(lambda e: e.tensor_tensor(sd_row, m_row, m_row, ALU.mult))
        vv = vstep(lambda e: e.scalar_tensor_tensor(s2m, s2m, EPS, sd_row, ALU.add, ALU.subtract))
        sch.wait("scalar", "vS", vv)
        sv = sch.op("scalar", lambda e: e.activation(sd_row, s2m, AF.Sqrt), "sS", 1)
        sch.wait("vector", "sS", sv)
        vstep(lambda e: e.reciprocal(a_row, sd_row))
        bv = vstep(lambda e: e.scalar_tensor_tensor(b_row, m_row, -1.0, a_row, ALU.mult, ALU.mult))
        sch.wait("tensor", "vS", bv)
        for tc in range(2):
            sch.op(
                "tensor",
                lambda e, tc=tc: e.matmul(
                    psA[:, tc * 512: tc * 512 + 512], ones_rf[:],
                    a_row[:, tc * 512: tc * 512 + 512], start=True, stop=True),
                None)
            bc = sch.op(
                "tensor",
                lambda e, tc=tc: e.matmul(
                    psB[:, tc * 512: tc * 512 + 512], ones_rf[:],
                    b_row[:, tc * 512: tc * 512 + 512], start=True, stop=True),
                "tS", 1)
        ln_v = 0
        ln_s = 0
        sch.wait("vector", "tS", bc)
        if write_bf:
            sch.wait("scalar", "tS", max(bc, s2_guard_t))
        for dt in range(NDT):
            for tc in range(2):
                sl = lambda dt=dt, tc=tc: resid[:, dt * 1024 + tc * 512: dt * 1024 + tc * 512 + 512]
                sch.op(
                    "vector",
                    lambda e, sl=sl, tc=tc: e.tensor_tensor(
                        t1f[:], sl(), psA[:, tc * 512: tc * 512 + 512], ALU.mult),
                    None)
                ln_v = sch.op(
                    "vector",
                    lambda e, sl=sl, tc=tc: e.tensor_tensor(
                        sl(), t1f[:], psB[:, tc * 512: tc * 512 + 512], ALU.add),
                    "vS", 1)
            if write_bf:
                sch.wait("scalar", "vS", ln_v)
                ln_s = sch.op(
                    "scalar",
                    lambda e, dt=dt: e.activation(
                        s2[:, dt * 1024:(dt + 1) * 1024],
                        resid[:, dt * 1024:(dt + 1) * 1024], AF.Copy),
                    "sS", 1)
        return ln_v, ln_s


def build_nc(stage=4):
    nc = bass.Bass("TRN2", target_bir_lowering=False, debug=False)

    xkvT = nc.dram_tensor("xkvT", [D, NSLOT], F32, kind="ExternalInput").ap()
    ropeA = nc.dram_tensor("ropeA", [128, NSLOT], F32, kind="ExternalInput").ap()
    ropeC = nc.dram_tensor("ropeC", [128, NSLOT], F32, kind="ExternalInput").ap()
    masks = nc.dram_tensor("masks", [128, 2 * NKT], F32, kind="ExternalInput").ap()
    trid = nc.dram_tensor("trid", [128, 4 * 512], F32, kind="ExternalInput").ap()
    identd = nc.dram_tensor("identd", [128, 128], F32, kind="ExternalInput").ap()
    bcols = nc.dram_tensor("bcols", [128, 48], F32, kind="ExternalInput").ap()
    bvrow = nc.dram_tensor("bvrow", [1, D], F32, kind="ExternalInput").ap()
    wqkv = nc.dram_tensor("wqkv", [D, 3 * D], F32, kind="ExternalInput").ap()
    wo = nc.dram_tensor("wo", [D, D], F32, kind="ExternalInput").ap()
    gatew = nc.dram_tensor("gatew", [D, NEXP], F32, kind="ExternalInput").ap()
    wg = nc.dram_tensor("wg", [NEXP, D, FEXP], F32, kind="ExternalInput").ap()
    wu = nc.dram_tensor("wu", [NEXP, D, FEXP], F32, kind="ExternalInput").ap()
    wd = nc.dram_tensor("wd", [NEXP, FEXP, D], F32, kind="ExternalInput").ap()
    sgw = nc.dram_tensor("sgw", [D, FSH], F32, kind="ExternalInput").ap()
    suw = nc.dram_tensor("suw", [D, FSH], F32, kind="ExternalInput").ap()
    sdw = nc.dram_tensor("sdw", [FSH, D], F32, kind="ExternalInput").ap()

    y = nc.dram_tensor("y", [OWN, D], F32, kind="ExternalOutput").ap()

    kindKV = "ExternalOutput" if stage == 1 else "Internal"
    ks = nc.dram_tensor("ks", [D, NSLOT], BF16, kind=kindKV).ap()
    vs = nc.dram_tensor("vs", [NSLOT, D], BF16, kind=kindKV).ap()
    cmbd = nc.dram_tensor("cmbd", [NEXP, OWN], F32,
                          kind="ExternalOutput" if stage == 3 else "Internal").ap()
    dbg_s2 = dbg_res = dbg_act = None
    if stage in (1, 2, 3):
        dbg_s2 = nc.dram_tensor("dbg_s2", [128, NDT * 1024], BF16, kind="ExternalOutput").ap()
        dbg_res = nc.dram_tensor("dbg_res", [128, NDT * 1024], F32, kind="ExternalOutput").ap()
    if stage == 3:
        dbg_act = nc.dram_tensor("dbg_act", [128, 2 * 6 * 1024], BF16, kind="ExternalOutput").ap()

    sch = Sched()
    es = ExitStack()
    with es:
        sb = lambda name, shape, dt_: es.enter_context(nc.sbuf_tensor(name, shape, dt_))
        s2 = sb("s2", [128, NDT * 1024], BF16)
        ones_cf = sb("ones_cf", [128, 1], F32)
        ones_cb = sb("ones_cb", [128, 1], BF16)
        ones_rf = sb("ones_rf", [1, 128], F32)
        ident = sb("ident", [128, 128], F32)
        bcol_sb = sb("bcol_sb", [128, 48], F32)

        v_bcol = sch.op("sync", lambda e: e.dma_start(bcol_sb[:], bcols), "dS", 16)
        v_id = sch.op("sync", lambda e: e.dma_start(ident[:], identd), "dS", 16)
        sch.op("vector", lambda e: e.memset(ones_cf[:], 1.0), None)
        sch.op("vector", lambda e: e.memset(ones_cb[:], 1.0), None)
        v_ones = sch.op("vector", lambda e: e.memset(ones_rf[:], 1.0), "vS", 1)

        s2_q_ready = 0

        with ExitStack() as p1:
            sb1 = lambda name, shape, dt_: p1.enter_context(nc.sbuf_tensor(name, shape, dt_))
            vbias = sb1("vbias", [128, D], F32)
            bvrow_sb = sb1("bvrow_sb", [1, D], F32)
            xb = sb1("xb", [128, NDT * GRP], BF16)
            wkq = [sb1(f"wkq{i}", [128, NDT * 128], BF16) for i in range(2)]
            wvb = [sb1(f"wvb{i}", [128, NDT * 256], BF16) for i in range(2)]
            rA = [sb1(f"rA{i}", [128, GRP], BF16) for i in range(2)]
            rC = [sb1(f"rC{i}", [128, GRP], BF16) for i in range(2)]
            kbf = [sb1(f"kbf{i}", [128, 512], F32) for i in range(2)]
            t1f = sb1("t1f", [128, 512], F32)
            t2f = sb1("t2f", [128, 512], F32)
            ksb = [sb1(f"ksb{i}", [128, 512], BF16) for i in range(2)]
            vsb = [sb1(f"vsb{i}", [128, 256], BF16) for i in range(2)]

            psK = [p1.enter_context(nc.psum_tensor(f"psK{i}", [128, 512], F32)) for i in range(2)]
            psV = [p1.enter_context(nc.psum_tensor(f"psV{i}", [128, 256], F32)) for i in range(2)]

            v_bvr = sch.op("sync", lambda e: e.dma_start(bvrow_sb[:], bvrow), "dS", 16)
            sch.wait("tensor", "dS", v_bvr)
            sch.wait("tensor", "vS", v_ones)
            last_cp = 0
            for dc in range(4):
                vmm = sch.op(
                    "tensor",
                    lambda e, dc=dc: e.matmul(psK[0][:], ones_rf[:],
                                              bvrow_sb[:, dc * 512:(dc + 1) * 512],
                                              start=True, stop=True),
                    "tS", 1)
                sch.wait("vector", "tS", vmm)
                last_cp = sch.op(
                    "vector",
                    lambda e, dc=dc: e.tensor_copy(vbias[:, dc * 512:(dc + 1) * 512], psK[0][:]),
                    "vS", 1)
                if dc < 3:
                    sch.wait("tensor", "vS", last_cp)

            xb_users_t = 0
            wkq_free = [0, 0]
            wvb_free = [0, 0]
            rope_free = [0, 0]
            psK_free = [0, 0]
            psK_free_vS = [last_cp, 0]
            psV_free = [0, 0]
            kb_free = [0, 0]
            ksb_free = [0, 0]
            vsb_free = [0, 0]
            uk = 0
            uv = 0

            first_k = True
            for g in range(NGRP):
                sl0 = g * GRP
                gp = g % 2
                sch.wait("gpsimd", "tS", xb_users_t)
                xb_v = sch.op(
                    "gpsimd",
                    lambda e, sl0=sl0: e.dma_start(
                        xb[:].rearrange("p (dt n) -> p dt n", n=GRP),
                        xkvT.rearrange("(dt p) s -> p dt s", p=128)[:, :, sl0:sl0 + GRP]),
                    "dG", 16)
                sch.wait("gpsimd", "vS", rope_free[gp])
                rA_v = sch.op(
                    "gpsimd",
                    lambda e, sl0=sl0, gp=gp: e.dma_start(rA[gp][:], ropeA[:, sl0:sl0 + GRP]),
                    "dG", 16)
                rC_v = sch.op(
                    "gpsimd",
                    lambda e, sl0=sl0, gp=gp: e.dma_start(rC[gp][:], ropeC[:, sl0:sl0 + GRP]),
                    "dG", 16)

                def kq_unit(h, c3, is_q, w_v, wpar):
                    nonlocal uk, s2_q_ready, first_k
                    par = uk % 2
                    sch.wait("tensor", "dG", max(xb_v, w_v))
                    sch.wait("tensor", "sS", psK_free[par])
                    if first_k:
                        sch.wait("tensor", "vS", psK_free_vS[0])
                        first_k = False
                    for dt in range(NDT):
                        mm = sch.op(
                            "tensor",
                            lambda e, par=par, wpar=wpar, dt=dt, c3=c3: e.matmul(
                                psK[par][:],
                                wkq[wpar][:, dt * 128:(dt + 1) * 128],
                                xb[:, dt * GRP + c3 * 512: dt * GRP + c3 * 512 + 512],
                                start=(dt == 0), stop=(dt == NDT - 1)),
                            "tS" if dt == NDT - 1 else None, 1)
                    sch.wait("scalar", "tS", mm)
                    sch.wait("scalar", "vS", kb_free[par])
                    bc = h if is_q else 16 + h
                    ev = sch.op(
                        "scalar",
                        lambda e, par=par, bc=bc: e.activation(
                            kbf[par][:], psK[par][:], AF.Identity, bias=bcol_sb[:, bc:bc + 1]),
                        "sS", 1)
                    psK_free[par] = ev
                    co = c3 * 512
                    sch.wait("vector", "sS", ev)
                    sch.wait("vector", "dG", max(rA_v, rC_v))
                    sch.op(
                        "vector",
                        lambda e, par=par, gp=gp, co=co: e.tensor_tensor(
                            t1f[:], kbf[par][:], rA[gp][:, co:co + 512], ALU.mult),
                        None)
                    sch.op(
                        "vector",
                        lambda e, par=par: e.tensor_copy(t2f[0:64, :], kbf[par][64:128, :]),
                        None)
                    sch.op(
                        "vector",
                        lambda e, par=par: e.tensor_copy(t2f[64:128, :], kbf[par][0:64, :]),
                        None)
                    sch.op(
                        "vector",
                        lambda e, gp=gp, co=co: e.tensor_tensor(
                            t2f[:], t2f[:], rC[gp][:, co:co + 512], ALU.mult),
                        None)
                    if is_q:
                        qh = c3 - 1
                        rp = sch.op(
                            "vector",
                            lambda e, h=h, qh=qh: e.tensor_tensor(
                                s2[:, h * 1024 + qh * 512: h * 1024 + qh * 512 + 512],
                                t1f[:], t2f[:], ALU.add),
                            "vS", 1)
                        s2_q_ready = rp
                    else:
                        sch.wait("vector", "dS", ksb_free[par])
                        rp = sch.op(
                            "vector",
                            lambda e, par=par: e.tensor_tensor(ksb[par][:], t1f[:], t2f[:], ALU.add),
                            "vS", 1)
                        sch.wait("sync", "vS", rp)
                        dv = sch.op(
                            "sync",
                            lambda e, par=par, h=h, c0=sl0 + c3 * 512: e.dma_start(
                                ks[h * 128:(h + 1) * 128, c0:c0 + 512], ksb[par][:]),
                            "dS", 16)
                        ksb_free[par] = dv
                    kb_free[par] = rp
                    uk += 1
                    return sch.cnt["tS"]

                for h in range(NH):
                    wpar = h % 2
                    sch.wait("gpsimd", "tS", wkq_free[wpar])
                    wk_v = sch.op(
                        "gpsimd",
                        lambda e, h=h, wpar=wpar: e.dma_start(
                            wkq[wpar][:].rearrange("p (dt c) -> p dt c", c=128),
                            wqkv.rearrange("(dt p) c -> p dt c", p=128)
                            [:, :, h * 384 + 128: h * 384 + 256]),
                        "dG", 16)
                    for c3 in range(3):
                        wkq_free[wpar] = kq_unit(h, c3, False, wk_v, wpar)

                if g == 2:
                    for h in range(NH):
                        wpar = h % 2
                        sch.wait("gpsimd", "tS", wkq_free[wpar])
                        wq_v = sch.op(
                            "gpsimd",
                            lambda e, h=h, wpar=wpar: e.dma_start(
                                wkq[wpar][:].rearrange("p (dt c) -> p dt c", c=128),
                                wqkv.rearrange("(dt p) c -> p dt c", p=128)
                                [:, :, h * 384: h * 384 + 128]),
                            "dG", 16)
                        for c3 in (1, 2):
                            wkq_free[wpar] = kq_unit(h, c3, True, wq_v, wpar)

                rope_free[gp] = sch.cnt["vS"]

                for dc in range(8):
                    wpar = dc % 2
                    sch.wait("gpsimd", "tS", wvb_free[wpar])
                    for hh in range(2):
                        wv_v = sch.op(
                            "gpsimd",
                            lambda e, dc=dc, wpar=wpar, hh=hh: e.dma_start(
                                wvb[wpar][:].rearrange("p (dt h hd) -> p dt h hd", h=2, hd=128)
                                [:, :, hh, :],
                                wqkv.rearrange("(dt p) (h t hd) -> p dt h t hd", p=128, t=3, hd=128)
                                [:, :, dc * 2 + hh, 2, :]),
                            "dG", 16)
                    for st in range(12):
                        par = uv % 2
                        sch.wait("tensor", "dG", max(xb_v, wv_v))
                        sch.wait("tensor", "vS", psV_free[par])
                        for dt in range(NDT):
                            mm = sch.op(
                                "tensor",
                                lambda e, par=par, wpar=wpar, dt=dt, st=st: e.matmul(
                                    psV[par][:],
                                    xb[:, dt * GRP + st * 128: dt * GRP + st * 128 + 128],
                                    wvb[wpar][:, dt * 256:(dt + 1) * 256],
                                    start=(dt == 0), stop=(dt == NDT - 1)),
                                "tS" if dt == NDT - 1 else None, 1)
                        wvb_free[wpar] = mm
                        sch.wait("vector", "tS", mm)
                        sch.wait("vector", "dS", vsb_free[par])
                        evv = sch.op(
                            "vector",
                            lambda e, par=par, dc=dc: e.scalar_tensor_tensor(
                                vsb[par][:], psV[par][:], 1.0,
                                vbias[:, dc * 256:(dc + 1) * 256], ALU.mult, ALU.add),
                            "vS", 1)
                        psV_free[par] = evv
                        sch.wait("sync", "vS", evv)
                        dvv = sch.op(
                            "sync",
                            lambda e, par=par, r0=sl0 + st * 128, dc=dc: e.dma_start(
                                vs[r0:r0 + 128, dc * 256:(dc + 1) * 256], vsb[par][:]),
                            "dS", 16)
                        vsb_free[par] = dvv
                        uv += 1
                xb_users_t = sch.cnt["tS"]

        sch.barrier()

        if stage == 1:
            sch.op("sync", lambda e: e.dma_start(dbg_s2[:], s2[:]), "dS", 16)
            sch.wait("sync", "dS", sch.cnt["dS"])
            return _emit(nc, sch, es)

        resid = sb("resid", [128, NDT * 1024], F32)
        rs_v = sch.op(
            "sync",
            lambda e: e.dma_start(
                resid[:].rearrange("p (dt n) -> p dt n", n=1024),
                xkvT.rearrange("(dt p) s -> p dt s", p=128)[:, :, 3584:4608]),
            "dS", 16)

        oproj_last = 0
        ra_last = 0
        with ExitStack() as p2:
            sb2 = lambda name, shape, dt_: p2.enter_context(nc.sbuf_tensor(name, shape, dt_))
            mask_sb = sb2("mask_sb", [128, 2 * NKT], F32)
            tri_sb = sb2("tri_sb", [128, 4 * 512], F32)
            Kh = [sb2(f"Kh{i}", [128, NSLOT], BF16) for i in range(2)]
            Vh = [sb2(f"Vh{i}", [128, NSLOT], BF16) for i in range(2)]
            Pt = [sb2(f"Pt{i}", [128, 512], BF16) for i in range(2)]
            rb = sb2("rb", [128, 512], F32)
            rl = sb2("rl", [1, 512], F32)
            wob = [sb2(f"wob{i}", [128, NDT * 128], BF16) for i in range(2)]
            tmpO = sb2("tmpO", [128, 512], F32)

            v_mask = sch.op("sync", lambda e: e.dma_start(mask_sb[:], masks), "dS", 16)
            v_tri = sch.op("sync", lambda e: e.dma_start(tri_sb[:], trid), "dS", 16)

            with ExitStack() as p2a:
                psS = [p2a.enter_context(nc.psum_tensor(f"psS{i}", [128, 512], F32)) for i in range(2)]
                psPV = p2a.enter_context(nc.psum_tensor("psPV", [128, 512], F32))
                psL = p2a.enter_context(nc.psum_tensor("psL", [1, 512], F32))
                psRB = p2a.enter_context(nc.psum_tensor("psRB", [128, 512], F32))
                psO = [p2a.enter_context(nc.psum_tensor(f"psO{i}", [128, 512], F32)) for i in range(2)]

                KVh_free = [0, 0]
                kv_v = [0, 0]
                psS_free = [0, 0]
                Pt_free = [0, 0]
                psPV_free = 0
                psL_free = 0
                att_done_v = 0

                def load_kv(h):
                    par = h % 2
                    sch.wait("sync", "tS", KVh_free[par])
                    sch.op(
                        "sync",
                        lambda e, h=h, par=par: e.dma_start(
                            Kh[par][:], ks[h * 128:(h + 1) * 128, :]),
                        "dS", 16)
                    vv = 0
                    for q4 in range(4):
                        vv = sch.op(
                            "sync",
                            lambda e, h=h, par=par, q4=q4: e.dma_start(
                                Vh[par][:, q4 * 9 * 128:(q4 + 1) * 9 * 128]
                                .rearrange("p (kt d) -> p kt d", d=128),
                                vs.rearrange("(kt p) d -> p kt d", p=128)
                                [:, q4 * 9:(q4 + 1) * 9, h * 128:(h + 1) * 128]),
                            "dS", 16)
                    kv_v[par] = vv

                load_kv(0)
                load_kv(1)
                for h in range(NH):
                    par = h % 2
                    for qc in range(2):
                        pv = 0
                        for kt in range(NKT):
                            sp = kt % 2
                            sch.wait("tensor", "dS", kv_v[par])
                            sch.wait("tensor", "sS", psS_free[sp])
                            if h == 0 and qc == 0 and kt == 0:
                                sch.wait("tensor", "vS", s2_q_ready)
                            smm = sch.op(
                                "tensor",
                                lambda e, par=par, sp=sp, kt=kt, h=h, qc=qc: e.matmul(
                                    psS[sp][:],
                                    Kh[par][:, kt * 128:(kt + 1) * 128],
                                    s2[:, h * 1024 + qc * 512: h * 1024 + qc * 512 + 512],
                                    start=True, stop=True),
                                "tS", 1)
                            diag = (qc == 0 and 28 <= kt < 32) or (qc == 1 and 32 <= kt < 36)
                            ew_sem, ew_val = "tS", smm
                            if diag:
                                tb = (kt - 28) if qc == 0 else (kt - 32)
                                sch.wait("vector", "tS", smm)
                                sch.wait("vector", "dS", v_tri)
                                tv = sch.op(
                                    "vector",
                                    lambda e, sp=sp, tb=tb: e.tensor_tensor(
                                        psS[sp][:], psS[sp][:],
                                        tri_sb[:, tb * 512:(tb + 1) * 512], ALU.add),
                                    "vS", 1)
                                ew_sem, ew_val = "vS", tv
                            mcol = qc * NKT + kt
                            sch.wait("scalar", ew_sem, ew_val)
                            sch.wait("scalar", "tS", Pt_free[sp])
                            if h == 0 and qc == 0 and kt == 0:
                                sch.wait("scalar", "dS", v_mask)
                            ex = sch.op(
                                "scalar",
                                lambda e, sp=sp, mcol=mcol: e.activation(
                                    Pt[sp][:], psS[sp][:], AF.Exp,
                                    bias=mask_sb[:, mcol:mcol + 1], scale=SCALE),
                                "sS", 1)
                            psS_free[sp] = ex
                            sch.wait("tensor", "sS", ex)
                            if kt == 0:
                                sch.wait("tensor", "vS", max(psL_free, psPV_free))
                            sch.op(
                                "tensor",
                                lambda e, sp=sp, kt=kt: e.matmul(
                                    psL[:], ones_cb[:], Pt[sp][:],
                                    start=(kt == 0), stop=(kt == NKT - 1)),
                                None)
                            pv = sch.op(
                                "tensor",
                                lambda e, par=par, sp=sp, kt=kt: e.matmul(
                                    psPV[:], Vh[par][:, kt * 128:(kt + 1) * 128], Pt[sp][:],
                                    start=(kt == 0), stop=(kt == NKT - 1)),
                                "tS", 1)
                            Pt_free[sp] = pv
                        KVh_free[par] = pv
                        sch.wait("vector", "tS", pv)
                        rcp = sch.op("vector", lambda e: e.reciprocal(rl[:], psL[:]), "vS", 1)
                        psL_free = rcp
                        sch.wait("tensor", "vS", rcp)
                        bmm = sch.op(
                            "tensor",
                            lambda e: e.matmul(psRB[:], ones_rf[:], rl[:], start=True, stop=True),
                            "tS", 1)
                        sch.wait("scalar", "tS", bmm)
                        cpv = sch.op("scalar", lambda e: e.copy(rb[:], psRB[:]), "sS", 1)
                        sch.wait("vector", "sS", cpv)
                        fin = sch.op(
                            "vector",
                            lambda e, h=h, qc=qc: e.tensor_tensor(
                                s2[:, h * 1024 + qc * 512: h * 1024 + qc * 512 + 512],
                                psPV[:], rb[:], ALU.mult),
                            "vS", 1)
                        psPV_free = fin
                        att_done_v = fin
                    if h + 2 < NH:
                        load_kv(h + 2)

                wo_v = [0, 0]
                wob_free = [0, 0]
                psO_free = [0, 0]

                def load_wo(dt):
                    wpar = dt % 2
                    sch.wait("gpsimd", "tS", wob_free[wpar])
                    wo_v[wpar] = sch.op(
                        "gpsimd",
                        lambda e, dt=dt: e.dma_start(
                            wob[dt % 2][:].rearrange("p (k c) -> p k c", c=128),
                            wo.rearrange("(k p) c -> p k c", p=128)[:, :, dt * 128:(dt + 1) * 128]),
                        "dG", 16)

                load_wo(0)
                load_wo(1)
                for dt in range(NDT):
                    wpar = dt % 2
                    for qc in range(2):
                        par = qc
                        sch.wait("tensor", "dG", wo_v[wpar])
                        sch.wait("tensor", "vS", att_done_v)
                        sch.wait("tensor", "sS", psO_free[par])
                        for k in range(NDT):
                            omm = sch.op(
                                "tensor",
                                lambda e, par=par, wpar=wpar, k=k, qc=qc: e.matmul(
                                    psO[par][:],
                                    wob[wpar][:, k * 128:(k + 1) * 128],
                                    s2[:, k * 1024 + qc * 512: k * 1024 + qc * 512 + 512],
                                    start=(k == 0), stop=(k == NDT - 1)),
                                "tS" if k == NDT - 1 else None, 1)
                        oproj_last = omm
                        sch.wait("scalar", "tS", omm)
                        ob = sch.op(
                            "scalar",
                            lambda e, par=par, dt=dt: e.activation(
                                tmpO[:], psO[par][:], AF.Identity, bias=bcol_sb[:, 32 + dt:33 + dt]),
                            "sS", 1)
                        psO_free[par] = ob
                        sch.wait("vector", "sS", ob)
                        if dt == 0 and qc == 0:
                            sch.wait("vector", "dS", rs_v)
                        ra_last = sch.op(
                            "vector",
                            lambda e, dt=dt, qc=qc: e.tensor_tensor(
                                resid[:, dt * 1024 + qc * 512: dt * 1024 + qc * 512 + 512],
                                resid[:, dt * 1024 + qc * 512: dt * 1024 + qc * 512 + 512],
                                tmpO[:], ALU.add),
                            "vS", 1)
                        sch.wait("scalar", "vS", ra_last)
                    wob_free[wpar] = oproj_last
                    if dt + 2 < NDT:
                        load_wo(dt + 2)

            _ln(nc, sch, resid, s2, ones_cb, ones_rf, ra_last, True, s2_guard_t=oproj_last)

        sch.barrier()

        if stage == 2:
            sch.op("sync", lambda e: e.dma_start(dbg_s2[:], s2[:]), "dS", 16)
            sch.op("sync", lambda e: e.dma_start(dbg_res[:], resid[:]), "dS", 16)
            sch.wait("sync", "dS", sch.cnt["dS"])
            return _emit(nc, sch, es)

        moe_done_v = 0
        with ExitStack() as p3:
            sb3 = lambda name, shape, dt_: p3.enter_context(nc.sbuf_tensor(name, shape, dt_))
            gate_sb = sb3("gate_sb", [128, NDT * NEXP], F32)
            sc_sb = sb3("sc_sb", [128, 8 * NEXP], F32)
            wk_sb = sb3("wk_sb", [128, NEXP], F32)
            pen_sb = sb3("pen_sb", [128, NEXP], F32)
            mtop = sb3("mtop", [128, 8], F32)
            mcur = sb3("mcur", [128, 1], F32)
            rowtmp = sb3("rowtmp", [128, 4], F32)
            cmb_tok = sb3("cmb_tok", [128, NEXP], F32)
            cmbT = sb3("cmbT", [NEXP, OWN], F32)
            cmbrow = sb3("cmbrow", [1, 2 * OWN], F32)
            cmbbc = [sb3(f"cmbbc{i}", [128, OWN], F32) for i in range(2)]
            act_sb = [sb3(f"act{i}", [128, 6 * 1024], BF16) for i in range(4)]
            dbg_act_src = act_sb
            wgb = [sb3(f"wgb{i}", [128, NDT * 128], BF16) for i in range(2)]
            wub = [sb3(f"wub{i}", [128, NDT * 128], BF16) for i in range(2)]
            wdb = [sb3(f"wdb{i}", [128, 4 * 6 * 128], BF16) for i in range(2)]
            sgf = [sb3(f"sgf{i}", [128, 512], F32) for i in range(2)]
            prf = sb3("prf", [128, 512], F32)

            v_gate = sch.op(
                "sync",
                lambda e: e.dma_start(
                    gate_sb[:].rearrange("p (dt c) -> p dt c", c=NEXP),
                    gatew.rearrange("(dt p) c -> p dt c", p=128)),
                "dS", 16)

            cmb_dma_done = 0
            with ExitStack() as p3a:
                psG = p3a.enter_context(nc.psum_tensor("psG", [128, NEXP], F32))
                psT = p3a.enter_context(nc.psum_tensor("psT", [NEXP, 128], F32))
                psG_free = 0
                psT_free = 0
                ct_last = 0
                for tt in range(8):
                    sch.wait("tensor", "dS", v_gate)
                    sch.wait("tensor", "sS", psG_free)
                    for dt in range(NDT):
                        gm = sch.op(
                            "tensor",
                            lambda e, dt=dt, tt=tt: e.matmul(
                                psG[:],
                                resid[:, dt * 1024 + tt * 128: dt * 1024 + tt * 128 + 128],
                                gate_sb[:, dt * NEXP:(dt + 1) * NEXP],
                                start=(dt == 0), stop=(dt == NDT - 1)),
                            "tS" if dt == NDT - 1 else None, 1)
                    sch.wait("scalar", "tS", gm)
                    sg_v = sch.op(
                        "scalar",
                        lambda e, tt=tt: e.activation(
                            sc_sb[:, tt * NEXP:(tt + 1) * NEXP], psG[:], AF.Sigmoid),
                        "sS", 1)
                    psG_free = sg_v
                    S0 = sc_sb[:, tt * NEXP:(tt + 1) * NEXP]
                    sch.wait("vector", "sS", sg_v)

                    def vstep(fn):
                        v = sch.op("vector", fn, "vS", 1)
                        sch.wait("vector", "vS", v)
                        return v

                    vstep(lambda e, S0=S0: e.tensor_copy(wk_sb[:], S0))
                    for k in range(TOPK):
                        vstep(lambda e: e.reduce_max(mcur[:], wk_sb[:], axis=AX.X))
                        vstep(lambda e, k=k: e.tensor_copy(mtop[:, k:k + 1], mcur[:]))
                        if k < TOPK - 1:
                            vstep(lambda e: e.tensor_tensor(
                                pen_sb[:], wk_sb[:], mcur[:].broadcast_to([128, NEXP]),
                                ALU.is_ge))
                            vstep(lambda e: e.tensor_scalar(
                                pen_sb[:], pen_sb[:], -1.0e30, None, ALU.mult))
                            vstep(lambda e: e.tensor_tensor(
                                wk_sb[:], wk_sb[:], pen_sb[:], ALU.add))
                    vstep(lambda e: e.reduce_sum(rowtmp[:, 0:1], mtop[:, 0:TOPK], axis=AX.X))
                    vstep(lambda e: e.tensor_scalar(rowtmp[:, 1:2], rowtmp[:, 0:1], 1e-20,
                                                    None, ALU.add))
                    vstep(lambda e: e.reciprocal(rowtmp[:, 2:3], rowtmp[:, 1:2]))
                    vstep(lambda e, S0=S0: e.tensor_tensor(
                        pen_sb[:], S0, mcur[:].broadcast_to([128, NEXP]), ALU.is_ge))
                    vstep(lambda e: e.tensor_tensor(
                        pen_sb[:], pen_sb[:], rowtmp[:, 2:3].broadcast_to([128, NEXP]),
                        ALU.mult))
                    cmv = vstep(lambda e, S0=S0: e.tensor_tensor(
                        cmb_tok[:], S0, pen_sb[:], ALU.mult))
                    sch.wait("tensor", "vS", max(cmv, psT_free))
                    sch.wait("tensor", "dS", v_id)
                    tr = sch.op(
                        "tensor",
                        lambda e: e.transpose(psT[:], cmb_tok[:], ident[:]),
                        "tS", 1)
                    sch.wait("vector", "tS", tr)
                    ct_last = sch.op(
                        "vector",
                        lambda e, tt=tt: e.tensor_copy(cmbT[:, tt * 128:(tt + 1) * 128], psT[:]),
                        "vS", 1)
                    psT_free = ct_last
                sch.wait("sync", "vS", ct_last)
                cmb_dma_done = sch.op("sync", lambda e: e.dma_start(cmbd[:], cmbT[:]), "dS", 16)

            groups = ([list(range(4 * i, 4 * i + 4)) for i in range(8)] + [[32, 33]]
                      if MOE_GROUPS_OVERRIDE is None else MOE_GROUPS_OVERRIDE)

            def wg_slice(e_, ft):
                if e_ < 32:
                    return wg[e_].rearrange("(dt p) c -> p dt c", p=128)[:, :, ft * 128:(ft + 1) * 128]
                off = (e_ - 32) * FEXP
                return sgw.rearrange("(dt p) c -> p dt c", p=128)[:, :, off + ft * 128: off + (ft + 1) * 128]

            def wu_slice(e_, ft):
                if e_ < 32:
                    return wu[e_].rearrange("(dt p) c -> p dt c", p=128)[:, :, ft * 128:(ft + 1) * 128]
                off = (e_ - 32) * FEXP
                return suw.rearrange("(dt p) c -> p dt c", p=128)[:, :, off + ft * 128: off + (ft + 1) * 128]

            def wd_slice(e_, dt):
                if e_ < 32:
                    return wd[e_].rearrange("(ft p) c -> p ft c", p=128)[:, :, dt * 128:(dt + 1) * 128]
                off = (e_ - 32) * FEXP
                return sdw[off:off + FEXP, :].rearrange("(ft p) c -> p ft c", p=128)[:, :, dt * 128:(dt + 1) * 128]

            with ExitStack() as p3b:
                psGg = [p3b.enter_context(nc.psum_tensor(f"psGg{i}", [128, 512], F32)) for i in range(2)]
                psUu = [p3b.enter_context(nc.psum_tensor(f"psUu{i}", [128, 512], F32)) for i in range(2)]
                psD = [p3b.enter_context(nc.psum_tensor(f"psD{i}", [128, 512], F32)) for i in range(2)]
                psCB = p3b.enter_context(nc.psum_tensor("psCB", [128, OWN], F32))

                gu_w_free = [0, 0]
                wdb_free = [0, 0]
                psGg_free = [0, 0]
                psUu_free = [0, 0]
                psD_free = [0, 0]
                sgf_free = [0, 0]
                cmbbc_free = [0, 0]
                cmbbc_ready = [0, 0]
                cmbrow_free = [0, 0]
                psCB_free = 0
                act_free = [0, 0, 0, 0]
                guv = 0
                ftv = 0

                for gi, grp in enumerate(groups):
                    for ei, e_ in enumerate(grp):
                        epar = e_ % 2
                        if e_ < 32:
                            sch.wait("sync", "dS", cmb_dma_done)
                            sch.wait("sync", "tS", cmbrow_free[epar])
                            crv = sch.op(
                                "sync",
                                lambda e, e_=e_, epar=epar: e.dma_start(
                                    cmbrow[:, epar * OWN:(epar + 1) * OWN], cmbd[e_:e_ + 1, :]),
                                "dS", 16)
                            sch.wait("tensor", "dS", crv)
                            sch.wait("tensor", "vS", psCB_free)
                            for tc in range(2):
                                cbm = sch.op(
                                    "tensor",
                                    lambda e, epar=epar, tc=tc: e.matmul(
                                        psCB[:, tc * 512: tc * 512 + 512], ones_rf[:],
                                        cmbrow[:, epar * OWN + tc * 512: epar * OWN + tc * 512 + 512],
                                        start=True, stop=True),
                                    "tS", 1)
                            cmbrow_free[epar] = cbm
                            sch.wait("vector", "tS", cbm)
                            sch.wait("vector", "vS", cmbbc_free[epar])
                            cbv = sch.op(
                                "vector",
                                lambda e, epar=epar: e.tensor_copy(cmbbc[epar][:], psCB[:]),
                                "vS", 1)
                            cmbbc_ready[epar] = cbv
                            psCB_free = cbv
                        for ft in range(6):
                            wpar = ftv % 2
                            ftv += 1
                            sch.wait("gpsimd", "tS", gu_w_free[wpar])
                            wgv = sch.op(
                                "gpsimd",
                                lambda e, e_=e_, ft=ft, wpar=wpar: e.dma_start(
                                    wgb[wpar][:].rearrange("p (dt c) -> p dt c", c=128),
                                    wg_slice(e_, ft)),
                                "dG", 16)
                            wuv = sch.op(
                                "gpsimd",
                                lambda e, e_=e_, ft=ft, wpar=wpar: e.dma_start(
                                    wub[wpar][:].rearrange("p (dt c) -> p dt c", c=128),
                                    wu_slice(e_, ft)),
                                "dG", 16)
                            for tc in range(2):
                                gpar = guv % 2
                                sch.wait("tensor", "dG", wuv)
                                sch.wait("tensor", "sS", psGg_free[gpar])
                                sch.wait("tensor", "vS", psUu_free[gpar])
                                for dt in range(NDT):
                                    gmm = sch.op(
                                        "tensor",
                                        lambda e, gpar=gpar, wpar=wpar, dt=dt, tc=tc: e.matmul(
                                            psGg[gpar][:],
                                            wgb[wpar][:, dt * 128:(dt + 1) * 128],
                                            s2[:, dt * 1024 + tc * 512: dt * 1024 + tc * 512 + 512],
                                            start=(dt == 0), stop=(dt == NDT - 1)),
                                        "tS" if dt == NDT - 1 else None, 1)
                                for dt in range(NDT):
                                    umm = sch.op(
                                        "tensor",
                                        lambda e, gpar=gpar, wpar=wpar, dt=dt, tc=tc: e.matmul(
                                            psUu[gpar][:],
                                            wub[wpar][:, dt * 128:(dt + 1) * 128],
                                            s2[:, dt * 1024 + tc * 512: dt * 1024 + tc * 512 + 512],
                                            start=(dt == 0), stop=(dt == NDT - 1)),
                                        "tS" if dt == NDT - 1 else None, 1)
                                gu_w_free[wpar] = umm
                                sch.wait("scalar", "tS", gmm)
                                sch.wait("scalar", "vS", sgf_free[gpar])
                                slv = sch.op(
                                    "scalar",
                                    lambda e, gpar=gpar: e.activation(
                                        sgf[gpar][:], psGg[gpar][:], AF.Silu),
                                    "sS", 1)
                                psGg_free[gpar] = slv
                                sch.wait("vector", "tS", umm)
                                sch.wait("vector", "sS", slv)
                                if ft == 0 and tc == 0:
                                    sch.wait("vector", "tS", act_free[ei])
                                if e_ < 32:
                                    sch.op(
                                        "vector",
                                        lambda e, gpar=gpar: e.tensor_tensor(
                                            prf[:], sgf[gpar][:], psUu[gpar][:], ALU.mult),
                                        None)
                                    if ft == 0 and tc == 0:
                                        sch.wait("vector", "vS", cmbbc_ready[epar])
                                    av = sch.op(
                                        "vector",
                                        lambda e, ei=ei, ft=ft, tc=tc, epar=epar: e.tensor_tensor(
                                            act_sb[ei][:, ft * 1024 + tc * 512: ft * 1024 + tc * 512 + 512],
                                            prf[:], cmbbc[epar][:, tc * 512: tc * 512 + 512], ALU.mult),
                                        "vS", 1)
                                else:
                                    av = sch.op(
                                        "vector",
                                        lambda e, ei=ei, ft=ft, tc=tc, gpar=gpar: e.tensor_tensor(
                                            act_sb[ei][:, ft * 1024 + tc * 512: ft * 1024 + tc * 512 + 512],
                                            sgf[gpar][:], psUu[gpar][:], ALU.mult),
                                        "vS", 1)
                                psUu_free[gpar] = av
                                sgf_free[gpar] = av
                                guv += 1
                        if e_ < 32:
                            cmbbc_free[epar] = av
                    acts_ready = sch.cnt["vS"]
                    ng = len(grp)
                    for dt in range(NDT):
                        dpar = dt % 2
                        sch.wait("gpsimd", "tS", wdb_free[dpar])
                        wdv = 0
                        for ei, e_ in enumerate(grp):
                            wdv = sch.op(
                                "gpsimd",
                                lambda e, e_=e_, dt=dt, ei=ei, dpar=dpar: e.dma_start(
                                    wdb[dpar][:, ei * 768:(ei + 1) * 768]
                                    .rearrange("p (ft c) -> p ft c", c=128),
                                    wd_slice(e_, dt)),
                                "dG", 16)
                        for tc in range(2):
                            ppar = tc
                            sch.wait("tensor", "dG", wdv)
                            sch.wait("tensor", "vS", max(acts_ready, psD_free[ppar]))
                            nmm = ng * 6
                            im = 0
                            for ei in range(ng):
                                for ft in range(6):
                                    dmm = sch.op(
                                        "tensor",
                                        lambda e, ppar=ppar, dpar=dpar, ei=ei, ft=ft, tc=tc,
                                        st_=(im == 0), sp_=(im == nmm - 1): e.matmul(
                                            psD[ppar][:],
                                            wdb[dpar][:, (ei * 6 + ft) * 128:(ei * 6 + ft) * 128 + 128],
                                            act_sb[ei][:, ft * 1024 + tc * 512: ft * 1024 + tc * 512 + 512],
                                            start=st_, stop=sp_),
                                        "tS" if im == nmm - 1 else None, 1)
                                    im += 1
                            sch.wait("vector", "tS", dmm)
                            moe_done_v = sch.op(
                                "vector",
                                lambda e, dt=dt, tc=tc, ppar=ppar: e.tensor_tensor(
                                    resid[:, dt * 1024 + tc * 512: dt * 1024 + tc * 512 + 512],
                                    resid[:, dt * 1024 + tc * 512: dt * 1024 + tc * 512 + 512],
                                    psD[ppar][:], ALU.add),
                                "vS", 1)
                            psD_free[ppar] = moe_done_v
                        wdb_free[dpar] = dmm
                    for ei in range(ng):
                        act_free[ei] = sch.cnt["tS"]

        sch.barrier()

        if stage == 3:
            sch.op("sync", lambda e: e.dma_start(dbg_s2[:], s2[:]), "dS", 16)
            sch.op("sync", lambda e: e.dma_start(dbg_res[:], resid[:]), "dS", 16)
            sch.op("sync", lambda e: e.dma_start(dbg_act[:, 0:6144], dbg_act_src[0][:]), "dS", 16)
            sch.op("sync", lambda e: e.dma_start(dbg_act[:, 6144:12288], dbg_act_src[1][:]), "dS", 16)
            sch.wait("sync", "dS", sch.cnt["dS"])
            return _emit(nc, sch, es)

        _ln(nc, sch, resid, None, ones_cb, ones_rf, moe_done_v, False, pfx="ln2")

        with ExitStack() as p4:
            sb4 = lambda name, shape, dt_: p4.enter_context(nc.sbuf_tensor(name, shape, dt_))
            osb = [sb4(f"osb{i}", [128, NDT * 128], F32) for i in range(2)]
            psX = [p4.enter_context(nc.psum_tensor(f"psX{i}", [128, 128], F32)) for i in range(2)]
            psX_free = [0, 0]
            osb_free = [0, 0]
            st_last = 0
            for tt in range(8):
                opar = tt % 2
                sch.wait("vector", "dS", osb_free[opar])
                for dt in range(NDT):
                    xpar = (tt * NDT + dt) % 2
                    sch.wait("tensor", "vS", psX_free[xpar])
                    sch.wait("tensor", "dS", v_id)
                    tr = sch.op(
                        "tensor",
                        lambda e, dt=dt, tt=tt, xpar=xpar: e.transpose(
                            psX[xpar][:],
                            resid[:, dt * 1024 + tt * 128: dt * 1024 + tt * 128 + 128],
                            ident[:]),
                        "tS", 1)
                    sch.wait("vector", "tS", tr)
                    cv = sch.op(
                        "vector",
                        lambda e, dt=dt, opar=opar, xpar=xpar: e.tensor_copy(
                            osb[opar][:, dt * 128:(dt + 1) * 128], psX[xpar][:]),
                        "vS", 1)
                    psX_free[xpar] = cv
                sch.wait("sync", "vS", cv)
                st_last = sch.op(
                    "sync",
                    lambda e, tt=tt, opar=opar: e.dma_start(
                        y[tt * 128:(tt + 1) * 128, :], osb[opar][:]),
                    "dS", 16)
                osb_free[opar] = st_last
            sch.wait("sync", "dS", st_last)

        return _emit(nc, sch, es)


def _emit(nc, sch, es):
    sems = {}
    for name in ("dG", "dS", "tS", "vS", "sS", "ph"):
        sems[name] = es.enter_context(nc.semaphore(name))

    block = es.enter_context(nc.Block())

    def run(engine_name):
        def f(eng):
            for step in sch.steps[engine_name]:
                if step[0] == "wait":
                    eng.wait_ge(sems[step[1]], step[2])
                else:
                    _, fn, inc, n = step
                    ins = fn(eng)
                    if inc is not None:
                        sem = sems["ph"] if inc in ("ph", "phx") else sems[inc]
                        ins.then_inc(sem, n if inc not in ("ph", "phx") else 1)
        return f

    block.tensor(run("tensor"))
    block.vector(run("vector"))
    block.scalar(run("scalar"))
    block.sync(run("sync"))
    block.gpsimd(run("gpsimd"))
    return nc



def make_core_inputs(ci, x, w_qkv, b_qkv, w_o, b_o, gate_w, wg_, wu_, wd_, sg_, su_, sd_):
    i = ci
    xs = [x[0, : i * 512], x[1, : (7 - i) * 512],
          x[0, i * 512:(i + 1) * 512], x[1, (7 - i) * 512:(8 - i) * 512]]
    xkv = np.concatenate(xs, axis=0)
    xkvT = np.ascontiguousarray(xkv.T)

    pos = np.concatenate([
        np.arange(i * 512), np.arange((7 - i) * 512),
        i * 512 + np.arange(512), (7 - i) * 512 + np.arange(512)]).astype(np.float64)
    inv = 1.0 / (10000.0 ** (np.arange(0, HD, 2, dtype=np.float64) / HD))
    invf = np.concatenate([inv, inv])
    emb = pos[None, :] * invf[:, None]
    ropeA_ = np.sin(emb).astype(np.float32)
    c = np.cos(emb)
    c[0:64, :] *= -1.0
    ropeC_ = c.astype(np.float32)

    batch_of_tile = np.zeros(NKT, dtype=np.int64)
    batch_of_tile[4 * i:28] = 1
    batch_of_tile[32:36] = 1
    mA = np.where(batch_of_tile == 0, 0.0, NEG).astype(np.float32)
    mB = np.where(batch_of_tile == 1, 0.0, NEG).astype(np.float32)
    masks_ = np.zeros((128, 2 * NKT), dtype=np.float32)
    masks_[:, 0:NKT] = mA[None, :]
    masks_[:, NKT:] = mB[None, :]

    tri = np.zeros((128, 4 * 512), dtype=np.float32)
    p_ = np.arange(128)
    q_ = np.arange(512)
    for k in range(4):
        tri[:, k * 512:(k + 1) * 512] = np.where(
            (k * 128 + p_[:, None]) > q_[None, :], NEG, 0.0)

    bcols_ = np.zeros((128, 48), dtype=np.float32)
    for h in range(NH):
        bcols_[:, h] = b_qkv[h * 384: h * 384 + 128]
        bcols_[:, 16 + h] = b_qkv[h * 384 + 128: h * 384 + 256]
    for dt in range(NDT):
        bcols_[:, 32 + dt] = b_o[dt * 128:(dt + 1) * 128]
    bvrow_ = np.zeros((1, D), dtype=np.float32)
    for h in range(NH):
        bvrow_[0, h * 128:(h + 1) * 128] = b_qkv[h * 384 + 256: h * 384 + 384]

    return dict(
        xkvT=xkvT, ropeA=ropeA_, ropeC=ropeC_, masks=masks_, trid=tri,
        identd=np.eye(128, dtype=np.float32), bcols=bcols_, bvrow=bvrow_,
        wqkv=w_qkv, wo=w_o, gatew=gate_w, wg=wg_, wu=wu_, wd=wd_,
        sgw=sg_, suw=su_, sdw=sd_)


def kernel(L0_attn_x, L0_attn_w_qkv, L0_attn_b_qkv, L0_attn_w_o, L0_attn_b_o,
           L0_ffn_gate_w, L0_ffn_wg, L0_ffn_wu, L0_ffn_wd, L0_ffn_sg, L0_ffn_su,
           L0_ffn_sd):
    args = [np.asarray(a, dtype=np.float32) for a in (
        L0_attn_x, L0_attn_w_qkv, L0_attn_b_qkv, L0_attn_w_o, L0_attn_b_o,
        L0_ffn_gate_w, L0_ffn_wg, L0_ffn_wu, L0_ffn_wd, L0_ffn_sg, L0_ffn_su,
        L0_ffn_sd)]
    x = args[0]
    in_maps = [make_core_inputs(i, x, *args[1:]) for i in range(NCORE)]
    nc = build_nc(stage=4)
    res = run_bass_kernel_spmd(nc, in_maps, core_ids=list(range(NCORE)))
    out = np.empty((B, S, D), dtype=np.float32)
    for i in range(NCORE):
        yi = res.results[i]["y"]
        out[0, i * 512:(i + 1) * 512] = yi[0:512]
        out[1, (7 - i) * 512:(8 - i) * 512] = yi[512:1024]
    return out


# version 08 (final)
# speedup vs baseline: 1.2559x; 1.2559x over previous
import math
from contextlib import ExitStack

import numpy as np
import ml_dtypes

import concourse.bass as bass
import concourse.mybir as mybir
from concourse.bass_utils import run_bass_kernel_spmd

F32 = mybir.dt.float32
BF16 = mybir.dt.bfloat16
AF = mybir.ActivationFunctionType
ALU = mybir.AluOpType
AX = mybir.AxisListType

B, S, D = 2, 4096, 2048
NH, HD = 16, 128
NCORE = 8
NSLOT = 4608
NKT = 36
OWN = 1024
NDT = 16
NEXP, FEXP, TOPK, FSH = 32, 768, 6, 1536
SCALE = 1.0 / math.sqrt(HD)
NEG = -9.0e15
EPS = 1e-5
GRP = 1536
NGRP = 3
MOE_GROUPS_OVERRIDE = None

ENGINES = ("tensor", "vector", "scalar", "sync", "gpsimd")


class Sched:

    def __init__(self):
        self.steps = {e: [] for e in ENGINES}
        self.cnt = {s: 0 for s in ("dG", "dS", "tS", "vS", "sS", "ph")}

    def wait(self, eng, sem, val):
        if val and val > 0:
            self.steps[eng].append(("wait", sem, val))

    def op(self, eng, fn, inc=None, n=1):
        v = None
        if inc is not None:
            self.cnt[inc] += n
            v = self.cnt[inc]
        self.steps[eng].append(("op", fn, inc, n))
        return v

    def barrier(self):
        self.wait("gpsimd", "dG", self.cnt["dG"])
        self.wait("sync", "dS", self.cnt["dS"])
        self.cnt["ph"] += len(ENGINES)
        tgt = self.cnt["ph"]
        for e in ENGINES:
            self.steps[e].append(("op", (lambda x: x.drain()), None, 1))
            self.steps[e].append(("op", (lambda x: x.nop()), "phx", 1))
            self.steps[e].append(("wait", "ph", tgt))
        return tgt


def _ln(nc, sch, resid, s2, ones_cb, ones_rf, resid_final_v, write_bf, s2_guard_t=0, pfx="ln"):
    with ExitStack() as ps:
        sbl = lambda name, shape, dt_: ps.enter_context(nc.sbuf_tensor(name, shape, dt_))
        stats = sbl(f"{pfx}_stats", [1, 5 * 1024], F32)
        bfs = [sbl(f"{pfx}_bfs{i}", [128, 1024], BF16) for i in range(2)]
        sqs = [sbl(f"{pfx}_sqs{i}", [128, 1024], BF16) for i in range(2)]
        t1f = sbl(f"{pfx}_t1f", [128, 512], F32)
        psStat = ps.enter_context(nc.psum_tensor(f"{pfx}_psStat", [1, 2048], F32))
        psA = ps.enter_context(nc.psum_tensor(f"{pfx}_psA", [128, 1024], F32))
        psB = ps.enter_context(nc.psum_tensor(f"{pfx}_psB", [128, 1024], F32))

        m_row = stats[:, 0:1024]
        s2m = stats[:, 1024:2048]
        a_row = stats[:, 2048:3072]
        b_row = stats[:, 3072:4096]
        sd_row = stats[:, 4096:5120]

        bfs_free = [0, 0]
        sqs_free = [0, 0]
        mm = 0
        for dt in range(NDT):
            par = dt % 2
            sch.wait("vector", "tS", bfs_free[par])
            if dt == 0:
                sch.wait("vector", "vS", resid_final_v)
            cv = sch.op(
                "vector",
                lambda e, dt=dt, par=par: e.tensor_copy(
                    bfs[par][:], resid[:, dt * 1024:(dt + 1) * 1024]),
                "vS", 1)
            sch.wait("scalar", "tS", sqs_free[par])
            sch.wait("scalar", "vS", resid_final_v)
            sv = sch.op(
                "scalar",
                lambda e, dt=dt, par=par: e.activation(
                    sqs[par][:], resid[:, dt * 1024:(dt + 1) * 1024], AF.Square),
                "sS", 1)
            sch.wait("tensor", "vS", cv)
            sch.wait("tensor", "sS", sv)
            for tc in range(2):
                sch.op(
                    "tensor",
                    lambda e, par=par, tc=tc, dt=dt: e.matmul(
                        psStat[0:1, tc * 512: tc * 512 + 512],
                        ones_cb[:], bfs[par][:, tc * 512: tc * 512 + 512],
                        start=(dt == 0), stop=(dt == NDT - 1)),
                    None)
                mm = sch.op(
                    "tensor",
                    lambda e, par=par, tc=tc, dt=dt: e.matmul(
                        psStat[0:1, 1024 + tc * 512: 1024 + tc * 512 + 512],
                        ones_cb[:], sqs[par][:, tc * 512: tc * 512 + 512],
                        start=(dt == 0), stop=(dt == NDT - 1)),
                    "tS", 1)
            bfs_free[par] = mm
            sqs_free[par] = mm

        def vstep(fn):
            v = sch.op("vector", fn, "vS", 1)
            sch.wait("vector", "vS", v)
            return v

        sch.wait("vector", "tS", mm)
        vstep(lambda e: e.tensor_scalar(m_row, psStat[0:1, 0:1024], 1.0 / D, None, ALU.mult))
        vstep(lambda e: e.tensor_scalar(s2m, psStat[0:1, 1024:2048], 1.0 / D, None, ALU.mult))
        vstep(lambda e: e.tensor_tensor(sd_row, m_row, m_row, ALU.mult))
        vv = vstep(lambda e: e.scalar_tensor_tensor(s2m, s2m, EPS, sd_row, ALU.add, ALU.subtract))
        sch.wait("scalar", "vS", vv)
        sv = sch.op("scalar", lambda e: e.activation(sd_row, s2m, AF.Sqrt), "sS", 1)
        sch.wait("vector", "sS", sv)
        vstep(lambda e: e.reciprocal(a_row, sd_row))
        bv = vstep(lambda e: e.scalar_tensor_tensor(b_row, m_row, -1.0, a_row, ALU.mult, ALU.mult))
        sch.wait("tensor", "vS", bv)
        for tc in range(2):
            sch.op(
                "tensor",
                lambda e, tc=tc: e.matmul(
                    psA[:, tc * 512: tc * 512 + 512], ones_rf[:],
                    a_row[:, tc * 512: tc * 512 + 512], start=True, stop=True),
                None)
            bc = sch.op(
                "tensor",
                lambda e, tc=tc: e.matmul(
                    psB[:, tc * 512: tc * 512 + 512], ones_rf[:],
                    b_row[:, tc * 512: tc * 512 + 512], start=True, stop=True),
                "tS", 1)
        ln_v = 0
        ln_s = 0
        sch.wait("vector", "tS", bc)
        if write_bf:
            sch.wait("scalar", "tS", max(bc, s2_guard_t))
        for dt in range(NDT):
            for tc in range(2):
                sl = lambda dt=dt, tc=tc: resid[:, dt * 1024 + tc * 512: dt * 1024 + tc * 512 + 512]
                sch.op(
                    "vector",
                    lambda e, sl=sl, tc=tc: e.tensor_tensor(
                        t1f[:], sl(), psA[:, tc * 512: tc * 512 + 512], ALU.mult),
                    None)
                ln_v = sch.op(
                    "vector",
                    lambda e, sl=sl, tc=tc: e.tensor_tensor(
                        sl(), t1f[:], psB[:, tc * 512: tc * 512 + 512], ALU.add),
                    "vS", 1)
            if write_bf:
                sch.wait("scalar", "vS", ln_v)
                ln_s = sch.op(
                    "scalar",
                    lambda e, dt=dt: e.activation(
                        s2[:, dt * 1024:(dt + 1) * 1024],
                        resid[:, dt * 1024:(dt + 1) * 1024], AF.Copy),
                    "sS", 1)
        return ln_v, ln_s


def build_nc(stage=4):
    nc = bass.Bass("TRN2", target_bir_lowering=False, debug=False)

    xkvT = nc.dram_tensor("xkvT", [D, NSLOT], F32, kind="ExternalInput").ap()
    ropeA = nc.dram_tensor("ropeA", [128, NSLOT], F32, kind="ExternalInput").ap()
    ropeC = nc.dram_tensor("ropeC", [128, NSLOT], F32, kind="ExternalInput").ap()
    masks = nc.dram_tensor("masks", [128, 2 * NKT], F32, kind="ExternalInput").ap()
    trid = nc.dram_tensor("trid", [128, 4 * 512], F32, kind="ExternalInput").ap()
    identd = nc.dram_tensor("identd", [128, 128], F32, kind="ExternalInput").ap()
    bcols = nc.dram_tensor("bcols", [128, 48], F32, kind="ExternalInput").ap()
    bvrow = nc.dram_tensor("bvrow", [1, D], F32, kind="ExternalInput").ap()
    wqkv = nc.dram_tensor("wqkv", [D, 3 * D], F32, kind="ExternalInput").ap()
    wo = nc.dram_tensor("wo", [D, D], F32, kind="ExternalInput").ap()
    gatew = nc.dram_tensor("gatew", [D, NEXP], F32, kind="ExternalInput").ap()
    wg = nc.dram_tensor("wg", [NEXP, D, FEXP], F32, kind="ExternalInput").ap()
    wu = nc.dram_tensor("wu", [NEXP, D, FEXP], F32, kind="ExternalInput").ap()
    wd = nc.dram_tensor("wd", [NEXP, FEXP, D], F32, kind="ExternalInput").ap()
    sgw = nc.dram_tensor("sgw", [D, FSH], F32, kind="ExternalInput").ap()
    suw = nc.dram_tensor("suw", [D, FSH], F32, kind="ExternalInput").ap()
    sdw = nc.dram_tensor("sdw", [FSH, D], F32, kind="ExternalInput").ap()

    y = nc.dram_tensor("y", [OWN, D], F32, kind="ExternalOutput").ap()

    kindKV = "ExternalOutput" if stage == 1 else "Internal"
    ks = nc.dram_tensor("ks", [D, NSLOT], BF16, kind=kindKV).ap()
    vs = nc.dram_tensor("vs", [NSLOT, D], BF16, kind=kindKV).ap()
    cmbd = nc.dram_tensor("cmbd", [NEXP, OWN], F32,
                          kind="ExternalOutput" if stage == 3 else "Internal").ap()
    dbg_s2 = dbg_res = dbg_act = None
    if stage in (1, 2, 3):
        dbg_s2 = nc.dram_tensor("dbg_s2", [128, NDT * 1024], BF16, kind="ExternalOutput").ap()
        dbg_res = nc.dram_tensor("dbg_res", [128, NDT * 1024], F32, kind="ExternalOutput").ap()
    if stage == 3:
        dbg_act = nc.dram_tensor("dbg_act", [128, 2 * 6 * 1024], BF16, kind="ExternalOutput").ap()

    sch = Sched()
    es = ExitStack()
    with es:
        sb = lambda name, shape, dt_: es.enter_context(nc.sbuf_tensor(name, shape, dt_))
        s2 = sb("s2", [128, NDT * 1024], BF16)
        ones_cf = sb("ones_cf", [128, 1], F32)
        ones_cb = sb("ones_cb", [128, 1], BF16)
        ones_rf = sb("ones_rf", [1, 128], F32)
        ident = sb("ident", [128, 128], F32)
        bcol_sb = sb("bcol_sb", [128, 48], F32)

        v_bcol = sch.op("sync", lambda e: e.dma_start(bcol_sb[:], bcols), "dS", 16)
        v_id = sch.op("sync", lambda e: e.dma_start(ident[:], identd), "dS", 16)
        sch.op("vector", lambda e: e.memset(ones_cf[:], 1.0), None)
        sch.op("vector", lambda e: e.memset(ones_cb[:], 1.0), None)
        v_ones = sch.op("vector", lambda e: e.memset(ones_rf[:], 1.0), "vS", 1)

        s2_q_ready = 0

        with ExitStack() as p1:
            sb1 = lambda name, shape, dt_: p1.enter_context(nc.sbuf_tensor(name, shape, dt_))
            vbias = sb1("vbias", [128, D], F32)
            bvrow_sb = sb1("bvrow_sb", [1, D], F32)
            xb = sb1("xb", [128, NDT * GRP], BF16)
            wkq = [sb1(f"wkq{i}", [128, NDT * 128], BF16) for i in range(2)]
            wvb = [sb1(f"wvb{i}", [128, NDT * 256], BF16) for i in range(2)]
            rA = [sb1(f"rA{i}", [128, GRP], BF16) for i in range(2)]
            rC = [sb1(f"rC{i}", [128, GRP], BF16) for i in range(2)]
            kbf = [sb1(f"kbf{i}", [128, 512], F32) for i in range(2)]
            t1f = sb1("t1f", [128, 512], F32)
            t2f = sb1("t2f", [128, 512], F32)
            ksb = [sb1(f"ksb{i}", [128, 512], BF16) for i in range(2)]
            vsb = [sb1(f"vsb{i}", [128, 256], BF16) for i in range(2)]

            psK = [p1.enter_context(nc.psum_tensor(f"psK{i}", [128, 512], F32)) for i in range(2)]
            psV = [p1.enter_context(nc.psum_tensor(f"psV{i}", [128, 256], F32)) for i in range(2)]

            v_bvr = sch.op("sync", lambda e: e.dma_start(bvrow_sb[:], bvrow), "dS", 16)
            sch.wait("tensor", "dS", v_bvr)
            sch.wait("tensor", "vS", v_ones)
            last_cp = 0
            for dc in range(4):
                vmm = sch.op(
                    "tensor",
                    lambda e, dc=dc: e.matmul(psK[0][:], ones_rf[:],
                                              bvrow_sb[:, dc * 512:(dc + 1) * 512],
                                              start=True, stop=True),
                    "tS", 1)
                sch.wait("vector", "tS", vmm)
                last_cp = sch.op(
                    "vector",
                    lambda e, dc=dc: e.tensor_copy(vbias[:, dc * 512:(dc + 1) * 512], psK[0][:]),
                    "vS", 1)
                if dc < 3:
                    sch.wait("tensor", "vS", last_cp)

            xb_users_t = 0
            wkq_free = [0, 0]
            wvb_free = [0, 0]
            rope_free = [0, 0]
            psK_free = [0, 0]
            psK_free_vS = [last_cp, 0]
            psV_free = [0, 0]
            kb_free = [0, 0]
            ksb_free = [0, 0]
            vsb_free = [0, 0]
            uk = 0
            uv = 0

            first_k = True
            for g in range(NGRP):
                sl0 = g * GRP
                gp = g % 2
                sch.wait("gpsimd", "tS", xb_users_t)
                xb_v = sch.op(
                    "gpsimd",
                    lambda e, sl0=sl0: e.dma_start(
                        xb[:].rearrange("p (dt n) -> p dt n", n=GRP),
                        xkvT.rearrange("(dt p) s -> p dt s", p=128)[:, :, sl0:sl0 + GRP]),
                    "dG", 16)
                sch.wait("gpsimd", "vS", rope_free[gp])
                rA_v = sch.op(
                    "gpsimd",
                    lambda e, sl0=sl0, gp=gp: e.dma_start(rA[gp][:], ropeA[:, sl0:sl0 + GRP]),
                    "dG", 16)
                rC_v = sch.op(
                    "gpsimd",
                    lambda e, sl0=sl0, gp=gp: e.dma_start(rC[gp][:], ropeC[:, sl0:sl0 + GRP]),
                    "dG", 16)

                def kq_unit(h, c3, is_q, w_v, wpar):
                    nonlocal uk, s2_q_ready, first_k
                    par = uk % 2
                    sch.wait("tensor", "dG", max(xb_v, w_v))
                    sch.wait("tensor", "sS", psK_free[par])
                    if first_k:
                        sch.wait("tensor", "vS", psK_free_vS[0])
                        first_k = False
                    for dt in range(NDT):
                        mm = sch.op(
                            "tensor",
                            lambda e, par=par, wpar=wpar, dt=dt, c3=c3: e.matmul(
                                psK[par][:],
                                wkq[wpar][:, dt * 128:(dt + 1) * 128],
                                xb[:, dt * GRP + c3 * 512: dt * GRP + c3 * 512 + 512],
                                start=(dt == 0), stop=(dt == NDT - 1)),
                            "tS" if dt == NDT - 1 else None, 1)
                    sch.wait("scalar", "tS", mm)
                    sch.wait("scalar", "vS", kb_free[par])
                    bc = h if is_q else 16 + h
                    ev = sch.op(
                        "scalar",
                        lambda e, par=par, bc=bc: e.activation(
                            kbf[par][:], psK[par][:], AF.Identity, bias=bcol_sb[:, bc:bc + 1]),
                        "sS", 1)
                    psK_free[par] = ev
                    co = c3 * 512
                    sch.wait("vector", "sS", ev)
                    sch.wait("vector", "dG", max(rA_v, rC_v))
                    sch.op(
                        "vector",
                        lambda e, par=par, gp=gp, co=co: e.tensor_tensor(
                            t1f[:], kbf[par][:], rA[gp][:, co:co + 512], ALU.mult),
                        None)
                    sch.op(
                        "vector",
                        lambda e, par=par: e.tensor_copy(t2f[0:64, :], kbf[par][64:128, :]),
                        None)
                    sch.op(
                        "vector",
                        lambda e, par=par: e.tensor_copy(t2f[64:128, :], kbf[par][0:64, :]),
                        None)
                    sch.op(
                        "vector",
                        lambda e, gp=gp, co=co: e.tensor_tensor(
                            t2f[:], t2f[:], rC[gp][:, co:co + 512], ALU.mult),
                        None)
                    if is_q:
                        qh = c3 - 1
                        rp = sch.op(
                            "vector",
                            lambda e, h=h, qh=qh: e.tensor_tensor(
                                s2[:, h * 1024 + qh * 512: h * 1024 + qh * 512 + 512],
                                t1f[:], t2f[:], ALU.add),
                            "vS", 1)
                        s2_q_ready = rp
                    else:
                        sch.wait("vector", "dS", ksb_free[par])
                        rp = sch.op(
                            "vector",
                            lambda e, par=par: e.tensor_tensor(ksb[par][:], t1f[:], t2f[:], ALU.add),
                            "vS", 1)
                        sch.wait("sync", "vS", rp)
                        dv = sch.op(
                            "sync",
                            lambda e, par=par, h=h, c0=sl0 + c3 * 512: e.dma_start(
                                ks[h * 128:(h + 1) * 128, c0:c0 + 512], ksb[par][:]),
                            "dS", 16)
                        ksb_free[par] = dv
                    kb_free[par] = rp
                    uk += 1
                    return sch.cnt["tS"]

                for h in range(NH):
                    wpar = h % 2
                    sch.wait("gpsimd", "tS", wkq_free[wpar])
                    wk_v = sch.op(
                        "gpsimd",
                        lambda e, h=h, wpar=wpar: e.dma_start(
                            wkq[wpar][:].rearrange("p (dt c) -> p dt c", c=128),
                            wqkv.rearrange("(dt p) c -> p dt c", p=128)
                            [:, :, h * 384 + 128: h * 384 + 256]),
                        "dG", 16)
                    for c3 in range(3):
                        wkq_free[wpar] = kq_unit(h, c3, False, wk_v, wpar)

                if g == 2:
                    for h in range(NH):
                        wpar = h % 2
                        sch.wait("gpsimd", "tS", wkq_free[wpar])
                        wq_v = sch.op(
                            "gpsimd",
                            lambda e, h=h, wpar=wpar: e.dma_start(
                                wkq[wpar][:].rearrange("p (dt c) -> p dt c", c=128),
                                wqkv.rearrange("(dt p) c -> p dt c", p=128)
                                [:, :, h * 384: h * 384 + 128]),
                            "dG", 16)
                        for c3 in (1, 2):
                            wkq_free[wpar] = kq_unit(h, c3, True, wq_v, wpar)

                rope_free[gp] = sch.cnt["vS"]

                for dc in range(8):
                    wpar = dc % 2
                    sch.wait("gpsimd", "tS", wvb_free[wpar])
                    for hh in range(2):
                        wv_v = sch.op(
                            "gpsimd",
                            lambda e, dc=dc, wpar=wpar, hh=hh: e.dma_start(
                                wvb[wpar][:].rearrange("p (dt h hd) -> p dt h hd", h=2, hd=128)
                                [:, :, hh, :],
                                wqkv.rearrange("(dt p) (h t hd) -> p dt h t hd", p=128, t=3, hd=128)
                                [:, :, dc * 2 + hh, 2, :]),
                            "dG", 16)
                    for st in range(12):
                        par = uv % 2
                        sch.wait("tensor", "dG", max(xb_v, wv_v))
                        sch.wait("tensor", "vS", psV_free[par])
                        for dt in range(NDT):
                            mm = sch.op(
                                "tensor",
                                lambda e, par=par, wpar=wpar, dt=dt, st=st: e.matmul(
                                    psV[par][:],
                                    xb[:, dt * GRP + st * 128: dt * GRP + st * 128 + 128],
                                    wvb[wpar][:, dt * 256:(dt + 1) * 256],
                                    start=(dt == 0), stop=(dt == NDT - 1)),
                                "tS" if dt == NDT - 1 else None, 1)
                        wvb_free[wpar] = mm
                        sch.wait("vector", "tS", mm)
                        sch.wait("vector", "dS", vsb_free[par])
                        evv = sch.op(
                            "vector",
                            lambda e, par=par, dc=dc: e.scalar_tensor_tensor(
                                vsb[par][:], psV[par][:], 1.0,
                                vbias[:, dc * 256:(dc + 1) * 256], ALU.mult, ALU.add),
                            "vS", 1)
                        psV_free[par] = evv
                        sch.wait("sync", "vS", evv)
                        dvv = sch.op(
                            "sync",
                            lambda e, par=par, r0=sl0 + st * 128, dc=dc: e.dma_start(
                                vs[r0:r0 + 128, dc * 256:(dc + 1) * 256], vsb[par][:]),
                            "dS", 16)
                        vsb_free[par] = dvv
                        uv += 1
                xb_users_t = sch.cnt["tS"]

        sch.barrier()

        if stage == 1:
            sch.op("sync", lambda e: e.dma_start(dbg_s2[:], s2[:]), "dS", 16)
            sch.wait("sync", "dS", sch.cnt["dS"])
            return _emit(nc, sch, es)

        resid = sb("resid", [128, NDT * 1024], F32)
        rs_v = sch.op(
            "sync",
            lambda e: e.dma_start(
                resid[:].rearrange("p (dt n) -> p dt n", n=1024),
                xkvT.rearrange("(dt p) s -> p dt s", p=128)[:, :, 3584:4608]),
            "dS", 16)

        oproj_last = 0
        ra_last = 0
        with ExitStack() as p2:
            sb2 = lambda name, shape, dt_: p2.enter_context(nc.sbuf_tensor(name, shape, dt_))
            mask_sb = sb2("mask_sb", [128, 2 * NKT], F32)
            tri_sb = sb2("tri_sb", [128, 4 * 512], F32)
            Kh = [sb2(f"Kh{i}", [128, NSLOT], BF16) for i in range(2)]
            Vh = [sb2(f"Vh{i}", [128, NSLOT], BF16) for i in range(2)]
            Pt = [sb2(f"Pt{i}", [128, 512], BF16) for i in range(3)]
            rb = [sb2(f"rb{i}", [128, 512], F32) for i in range(2)]
            rl = [sb2(f"rl{i}", [1, 512], F32) for i in range(2)]
            wob = [sb2(f"wob{i}", [128, NDT * 128], BF16) for i in range(2)]
            tmpO = sb2("tmpO", [128, 512], F32)

            v_mask = sch.op("sync", lambda e: e.dma_start(mask_sb[:], masks), "dS", 16)
            v_tri = sch.op("sync", lambda e: e.dma_start(tri_sb[:], trid), "dS", 16)

            with ExitStack() as p2a:
                psS = [p2a.enter_context(nc.psum_tensor(f"psS{i}", [128, 512], F32)) for i in range(3)]
                psPV = [p2a.enter_context(nc.psum_tensor(f"psPV{i}", [128, 512], F32)) for i in range(2)]
                psL = [p2a.enter_context(nc.psum_tensor(f"psL{i}", [1, 512], F32)) for i in range(2)]
                psRB = p2a.enter_context(nc.psum_tensor("psRB", [128, 512], F32))

                KVh_free = [0, 0]
                kv_v = [0, 0]
                psS_free = [0, 0, 0]
                Pt_free = [0, 0, 0]
                psPV_free = [0, 0]
                psL_free = [0, 0]
                psRB_free = 0
                att_done_v = 0
                pend = []

                def load_kv(h):
                    par = h % 2
                    sch.wait("sync", "tS", KVh_free[par])
                    sch.op(
                        "sync",
                        lambda e, h=h, par=par: e.dma_start(
                            Kh[par][:], ks[h * 128:(h + 1) * 128, :]),
                        "dS", 16)
                    vv = 0
                    for q4 in range(4):
                        vv = sch.op(
                            "sync",
                            lambda e, h=h, par=par, q4=q4: e.dma_start(
                                Vh[par][:, q4 * 9 * 128:(q4 + 1) * 9 * 128]
                                .rearrange("p (kt d) -> p kt d", d=128),
                                vs.rearrange("(kt p) d -> p kt d", p=128)
                                [:, q4 * 9:(q4 + 1) * 9, h * 128:(h + 1) * 128]),
                            "dS", 16)
                    kv_v[par] = vv

                def fin_stage1(h_, qc_, up_, pv_):
                    sch.wait("vector", "tS", pv_)
                    rcp = sch.op(
                        "vector",
                        lambda e, up_=up_: e.reciprocal(rl[up_][:], psL[up_][:]), "vS", 1)
                    psL_free[up_] = rcp
                    return rcp

                def fin_stage2(up_, rcp):
                    nonlocal psRB_free
                    sch.wait("tensor", "vS", rcp)
                    sch.wait("tensor", "sS", psRB_free)
                    bmm = sch.op(
                        "tensor",
                        lambda e, up_=up_: e.matmul(psRB[:], ones_rf[:], rl[up_][:],
                                                    start=True, stop=True),
                        "tS", 1)
                    sch.wait("scalar", "tS", bmm)
                    cpv = sch.op(
                        "scalar",
                        lambda e, up_=up_: e.copy(rb[up_][:], psRB[:]), "sS", 1)
                    psRB_free = cpv
                    return cpv

                def fin_stage3(h_, qc_, up_, cpv):
                    sch.wait("vector", "sS", cpv)
                    fin = sch.op(
                        "vector",
                        lambda e, h_=h_, qc_=qc_, up_=up_: e.tensor_tensor(
                            s2[:, h_ * 1024 + qc_ * 512: h_ * 1024 + qc_ * 512 + 512],
                            psPV[up_][:], rb[up_][:], ALU.mult),
                        "vS", 1)
                    psPV_free[up_] = fin
                    return fin

                load_kv(0)
                load_kv(1)
                unit = 0
                for h in range(NH):
                    par = h % 2
                    for qc in range(2):
                        upar = unit % 2
                        pv = 0
                        LAG = 2
                        kts = [t for t in range(NKT)
                               if not (qc == 0 and 32 <= t < 36)
                               and not (qc == 1 and 28 <= t < 32)]
                        NT = len(kts)
                        exv = [0] * NT
                        for step in range(NT + LAG):
                            if pend and step == 2:
                                pend[0]["rcp"] = fin_stage1(pend[0]["h"], pend[0]["qc"],
                                                            pend[0]["up"], pend[0]["pv"])
                            if pend and step == 10:
                                pend[0]["cpv"] = fin_stage2(pend[0]["up"], pend[0]["rcp"])
                            if pend and step == 18:
                                st = pend.pop(0)
                                att_done_v = fin_stage3(st["h"], st["qc"], st["up"], st["cpv"])
                            if step < NT:
                                si = step
                                kt = kts[si]
                                sp = si % 3
                                sch.wait("tensor", "dS", kv_v[par])
                                sch.wait("tensor", "sS", psS_free[sp])
                                if h == 0 and qc == 0 and si == 0:
                                    sch.wait("tensor", "vS", s2_q_ready)
                                smm = sch.op(
                                    "tensor",
                                    lambda e, par=par, sp=sp, kt=kt, h=h, qc=qc: e.matmul(
                                        psS[sp][:],
                                        Kh[par][:, kt * 128:(kt + 1) * 128],
                                        s2[:, h * 1024 + qc * 512: h * 1024 + qc * 512 + 512],
                                        start=True, stop=True),
                                    "tS", 1)
                                diag = (qc == 0 and 28 <= kt < 32) or (qc == 1 and 32 <= kt < 36)
                                ew_sem, ew_val = "tS", smm
                                if diag:
                                    tb = (kt - 28) if qc == 0 else (kt - 32)
                                    sch.wait("vector", "tS", smm)
                                    sch.wait("vector", "dS", v_tri)
                                    tv = sch.op(
                                        "vector",
                                        lambda e, sp=sp, tb=tb: e.tensor_tensor(
                                            psS[sp][:], psS[sp][:],
                                            tri_sb[:, tb * 512:(tb + 1) * 512], ALU.add),
                                        "vS", 1)
                                    ew_sem, ew_val = "vS", tv
                                mcol = qc * NKT + kt
                                sch.wait("scalar", ew_sem, ew_val)
                                sch.wait("scalar", "tS", Pt_free[sp])
                                if h == 0 and qc == 0 and si == 0:
                                    sch.wait("scalar", "dS", v_mask)
                                exv[si] = sch.op(
                                    "scalar",
                                    lambda e, sp=sp, mcol=mcol: e.activation(
                                        Pt[sp][:], psS[sp][:], AF.Exp,
                                        bias=mask_sb[:, mcol:mcol + 1], scale=SCALE),
                                    "sS", 1)
                                psS_free[sp] = exv[si]
                            if step >= LAG:
                                si = step - LAG
                                kt = kts[si]
                                sp = si % 3
                                sch.wait("tensor", "sS", exv[si])
                                if si == 0:
                                    sch.wait("tensor", "vS", max(psL_free[upar], psPV_free[upar]))
                                sch.op(
                                    "tensor",
                                    lambda e, sp=sp, si=si, upar=upar, NT=NT: e.matmul(
                                        psL[upar][:], ones_cb[:], Pt[sp][:],
                                        start=(si == 0), stop=(si == NT - 1)),
                                    None)
                                pv = sch.op(
                                    "tensor",
                                    lambda e, par=par, sp=sp, kt=kt, si=si, upar=upar, NT=NT: e.matmul(
                                        psPV[upar][:], Vh[par][:, kt * 128:(kt + 1) * 128], Pt[sp][:],
                                        start=(si == 0), stop=(si == NT - 1)),
                                    "tS", 1)
                                Pt_free[sp] = pv
                        KVh_free[par] = pv
                        pend.append(dict(h=h, qc=qc, up=upar, pv=pv))
                        unit += 1
                    if h + 2 < NH:
                        load_kv(h + 2)
                while pend:
                    st = pend.pop(0)
                    rcp = fin_stage1(st["h"], st["qc"], st["up"], st["pv"])
                    cpv = fin_stage2(st["up"], rcp)
                    att_done_v = fin_stage3(st["h"], st["qc"], st["up"], cpv)

                wo_v = [0, 0]
                wob_free = [0, 0]

                def load_wo(dt):
                    wpar = dt % 2
                    sch.wait("gpsimd", "tS", wob_free[wpar])
                    wo_v[wpar] = sch.op(
                        "gpsimd",
                        lambda e, dt=dt: e.dma_start(
                            wob[dt % 2][:].rearrange("p (k c) -> p k c", c=128),
                            wo.rearrange("(k p) c -> p k c", p=128)[:, :, dt * 128:(dt + 1) * 128]),
                        "dG", 16)

                load_wo(0)
                load_wo(1)
                psO_free = [max(psS_free), max(psS_free)]
                for dt in range(NDT):
                    wpar = dt % 2
                    for qc in range(2):
                        par = qc
                        sch.wait("tensor", "dG", wo_v[wpar])
                        sch.wait("tensor", "vS", att_done_v)
                        sch.wait("tensor", "sS", psO_free[par])
                        for k in range(NDT):
                            omm = sch.op(
                                "tensor",
                                lambda e, par=par, wpar=wpar, k=k, qc=qc: e.matmul(
                                    psS[par][:],
                                    wob[wpar][:, k * 128:(k + 1) * 128],
                                    s2[:, k * 1024 + qc * 512: k * 1024 + qc * 512 + 512],
                                    start=(k == 0), stop=(k == NDT - 1)),
                                "tS" if k == NDT - 1 else None, 1)
                        oproj_last = omm
                        sch.wait("scalar", "tS", omm)
                        ob = sch.op(
                            "scalar",
                            lambda e, par=par, dt=dt: e.activation(
                                tmpO[:], psS[par][:], AF.Identity, bias=bcol_sb[:, 32 + dt:33 + dt]),
                            "sS", 1)
                        psO_free[par] = ob
                        sch.wait("vector", "sS", ob)
                        if dt == 0 and qc == 0:
                            sch.wait("vector", "dS", rs_v)
                        ra_last = sch.op(
                            "vector",
                            lambda e, dt=dt, qc=qc: e.tensor_tensor(
                                resid[:, dt * 1024 + qc * 512: dt * 1024 + qc * 512 + 512],
                                resid[:, dt * 1024 + qc * 512: dt * 1024 + qc * 512 + 512],
                                tmpO[:], ALU.add),
                            "vS", 1)
                        sch.wait("scalar", "vS", ra_last)
                    wob_free[wpar] = oproj_last
                    if dt + 2 < NDT:
                        load_wo(dt + 2)

            _ln(nc, sch, resid, s2, ones_cb, ones_rf, ra_last, True, s2_guard_t=oproj_last)

        sch.barrier()

        if stage == 2:
            sch.op("sync", lambda e: e.dma_start(dbg_s2[:], s2[:]), "dS", 16)
            sch.op("sync", lambda e: e.dma_start(dbg_res[:], resid[:]), "dS", 16)
            sch.wait("sync", "dS", sch.cnt["dS"])
            return _emit(nc, sch, es)

        moe_done_v = 0
        with ExitStack() as p3:
            sb3 = lambda name, shape, dt_: p3.enter_context(nc.sbuf_tensor(name, shape, dt_))
            gate_sb = sb3("gate_sb", [128, NDT * NEXP], F32)
            sc_sb = sb3("sc_sb", [128, 8 * NEXP], F32)
            wk_sb = sb3("wk_sb", [128, NEXP], F32)
            pen_sb = sb3("pen_sb", [128, NEXP], F32)
            mtop = sb3("mtop", [128, 8], F32)
            mcur = sb3("mcur", [128, 1], F32)
            rowtmp = sb3("rowtmp", [128, 4], F32)
            cmb_tok = sb3("cmb_tok", [128, NEXP], F32)
            cmbT = sb3("cmbT", [NEXP, OWN], F32)
            cmbrow = sb3("cmbrow", [1, 2 * OWN], F32)
            cmbbc = [sb3(f"cmbbc{i}", [128, OWN], F32) for i in range(2)]
            act_sb = [sb3(f"act{i}", [128, 6 * 1024], BF16) for i in range(4)]
            dbg_act_src = act_sb
            wgb = [sb3(f"wgb{i}", [128, NDT * 128], BF16) for i in range(2)]
            wub = [sb3(f"wub{i}", [128, NDT * 128], BF16) for i in range(2)]
            wdb = [sb3(f"wdb{i}", [128, 4 * 6 * 128], BF16) for i in range(2)]
            sgf = [sb3(f"sgf{i}", [128, 512], F32) for i in range(2)]
            prf = sb3("prf", [128, 512], F32)

            v_gate = sch.op(
                "sync",
                lambda e: e.dma_start(
                    gate_sb[:].rearrange("p (dt c) -> p dt c", c=NEXP),
                    gatew.rearrange("(dt p) c -> p dt c", p=128)),
                "dS", 16)

            cmb_dma_done = 0
            with ExitStack() as p3a:
                psG = p3a.enter_context(nc.psum_tensor("psG", [128, NEXP], F32))
                psT = p3a.enter_context(nc.psum_tensor("psT", [NEXP, 128], F32))
                psG_free = 0
                psT_free = 0
                ct_last = 0
                for tt in range(8):
                    sch.wait("tensor", "dS", v_gate)
                    sch.wait("tensor", "sS", psG_free)
                    for dt in range(NDT):
                        gm = sch.op(
                            "tensor",
                            lambda e, dt=dt, tt=tt: e.matmul(
                                psG[:],
                                resid[:, dt * 1024 + tt * 128: dt * 1024 + tt * 128 + 128],
                                gate_sb[:, dt * NEXP:(dt + 1) * NEXP],
                                start=(dt == 0), stop=(dt == NDT - 1)),
                            "tS" if dt == NDT - 1 else None, 1)
                    sch.wait("scalar", "tS", gm)
                    sg_v = sch.op(
                        "scalar",
                        lambda e, tt=tt: e.activation(
                            sc_sb[:, tt * NEXP:(tt + 1) * NEXP], psG[:], AF.Sigmoid),
                        "sS", 1)
                    psG_free = sg_v
                    S0 = sc_sb[:, tt * NEXP:(tt + 1) * NEXP]
                    sch.wait("vector", "sS", sg_v)

                    def vstep(fn):
                        v = sch.op("vector", fn, "vS", 1)
                        sch.wait("vector", "vS", v)
                        return v

                    vstep(lambda e, S0=S0: e.tensor_copy(wk_sb[:], S0))
                    for k in range(TOPK):
                        vstep(lambda e: e.reduce_max(mcur[:], wk_sb[:], axis=AX.X))
                        vstep(lambda e, k=k: e.tensor_copy(mtop[:, k:k + 1], mcur[:]))
                        if k < TOPK - 1:
                            vstep(lambda e: e.tensor_tensor(
                                pen_sb[:], wk_sb[:], mcur[:].broadcast_to([128, NEXP]),
                                ALU.is_ge))
                            vstep(lambda e: e.tensor_scalar(
                                pen_sb[:], pen_sb[:], -1.0e30, None, ALU.mult))
                            vstep(lambda e: e.tensor_tensor(
                                wk_sb[:], wk_sb[:], pen_sb[:], ALU.add))
                    vstep(lambda e: e.reduce_sum(rowtmp[:, 0:1], mtop[:, 0:TOPK], axis=AX.X))
                    vstep(lambda e: e.tensor_scalar(rowtmp[:, 1:2], rowtmp[:, 0:1], 1e-20,
                                                    None, ALU.add))
                    vstep(lambda e: e.reciprocal(rowtmp[:, 2:3], rowtmp[:, 1:2]))
                    vstep(lambda e, S0=S0: e.tensor_tensor(
                        pen_sb[:], S0, mcur[:].broadcast_to([128, NEXP]), ALU.is_ge))
                    vstep(lambda e: e.tensor_tensor(
                        pen_sb[:], pen_sb[:], rowtmp[:, 2:3].broadcast_to([128, NEXP]),
                        ALU.mult))
                    cmv = vstep(lambda e, S0=S0: e.tensor_tensor(
                        cmb_tok[:], S0, pen_sb[:], ALU.mult))
                    sch.wait("tensor", "vS", max(cmv, psT_free))
                    sch.wait("tensor", "dS", v_id)
                    tr = sch.op(
                        "tensor",
                        lambda e: e.transpose(psT[:], cmb_tok[:], ident[:]),
                        "tS", 1)
                    sch.wait("vector", "tS", tr)
                    ct_last = sch.op(
                        "vector",
                        lambda e, tt=tt: e.tensor_copy(cmbT[:, tt * 128:(tt + 1) * 128], psT[:]),
                        "vS", 1)
                    psT_free = ct_last
                sch.wait("sync", "vS", ct_last)
                cmb_dma_done = sch.op("sync", lambda e: e.dma_start(cmbd[:], cmbT[:]), "dS", 16)

            groups = ([list(range(4 * i, 4 * i + 4)) for i in range(8)] + [[32, 33]]
                      if MOE_GROUPS_OVERRIDE is None else MOE_GROUPS_OVERRIDE)

            def wg_slice(e_, ft):
                if e_ < 32:
                    return wg[e_].rearrange("(dt p) c -> p dt c", p=128)[:, :, ft * 128:(ft + 1) * 128]
                off = (e_ - 32) * FEXP
                return sgw.rearrange("(dt p) c -> p dt c", p=128)[:, :, off + ft * 128: off + (ft + 1) * 128]

            def wu_slice(e_, ft):
                if e_ < 32:
                    return wu[e_].rearrange("(dt p) c -> p dt c", p=128)[:, :, ft * 128:(ft + 1) * 128]
                off = (e_ - 32) * FEXP
                return suw.rearrange("(dt p) c -> p dt c", p=128)[:, :, off + ft * 128: off + (ft + 1) * 128]

            def wd_slice(e_, dt):
                if e_ < 32:
                    return wd[e_].rearrange("(ft p) c -> p ft c", p=128)[:, :, dt * 128:(dt + 1) * 128]
                off = (e_ - 32) * FEXP
                return sdw[off:off + FEXP, :].rearrange("(ft p) c -> p ft c", p=128)[:, :, dt * 128:(dt + 1) * 128]

            with ExitStack() as p3b:
                psGg = [p3b.enter_context(nc.psum_tensor(f"psGg{i}", [128, 512], F32)) for i in range(2)]
                psUu = [p3b.enter_context(nc.psum_tensor(f"psUu{i}", [128, 512], F32)) for i in range(2)]
                psD = [p3b.enter_context(nc.psum_tensor(f"psD{i}", [128, 512], F32)) for i in range(2)]
                psCB = p3b.enter_context(nc.psum_tensor("psCB", [128, OWN], F32))

                gu_w_free = [0, 0]
                wdb_free = [0, 0]
                psGg_free = [0, 0]
                psUu_free = [0, 0]
                psD_free = [0, 0]
                sgf_free = [0, 0]
                cmbbc_free = [0, 0]
                cmbbc_ready = [0, 0]
                cmbrow_free = [0, 0]
                psCB_free = 0
                act_free = [0, 0, 0, 0]
                guv = 0
                ftv = 0

                for gi, grp in enumerate(groups):
                    for ei, e_ in enumerate(grp):
                        epar = e_ % 2
                        if e_ < 32:
                            sch.wait("sync", "dS", cmb_dma_done)
                            sch.wait("sync", "tS", cmbrow_free[epar])
                            crv = sch.op(
                                "sync",
                                lambda e, e_=e_, epar=epar: e.dma_start(
                                    cmbrow[:, epar * OWN:(epar + 1) * OWN], cmbd[e_:e_ + 1, :]),
                                "dS", 16)
                            sch.wait("tensor", "dS", crv)
                            sch.wait("tensor", "vS", psCB_free)
                            for tc in range(2):
                                cbm = sch.op(
                                    "tensor",
                                    lambda e, epar=epar, tc=tc: e.matmul(
                                        psCB[:, tc * 512: tc * 512 + 512], ones_rf[:],
                                        cmbrow[:, epar * OWN + tc * 512: epar * OWN + tc * 512 + 512],
                                        start=True, stop=True),
                                    "tS", 1)
                            cmbrow_free[epar] = cbm
                            sch.wait("vector", "tS", cbm)
                            sch.wait("vector", "vS", cmbbc_free[epar])
                            cbv = sch.op(
                                "vector",
                                lambda e, epar=epar: e.tensor_copy(cmbbc[epar][:], psCB[:]),
                                "vS", 1)
                            cmbbc_ready[epar] = cbv
                            psCB_free = cbv
                        for ft in range(6):
                            wpar = ftv % 2
                            ftv += 1
                            sch.wait("gpsimd", "tS", gu_w_free[wpar])
                            wgv = sch.op(
                                "gpsimd",
                                lambda e, e_=e_, ft=ft, wpar=wpar: e.dma_start(
                                    wgb[wpar][:].rearrange("p (dt c) -> p dt c", c=128),
                                    wg_slice(e_, ft)),
                                "dG", 16)
                            wuv = sch.op(
                                "gpsimd",
                                lambda e, e_=e_, ft=ft, wpar=wpar: e.dma_start(
                                    wub[wpar][:].rearrange("p (dt c) -> p dt c", c=128),
                                    wu_slice(e_, ft)),
                                "dG", 16)
                            for tc in range(2):
                                gpar = guv % 2
                                sch.wait("tensor", "dG", wuv)
                                sch.wait("tensor", "sS", psGg_free[gpar])
                                sch.wait("tensor", "vS", psUu_free[gpar])
                                for dt in range(NDT):
                                    gmm = sch.op(
                                        "tensor",
                                        lambda e, gpar=gpar, wpar=wpar, dt=dt, tc=tc: e.matmul(
                                            psGg[gpar][:],
                                            wgb[wpar][:, dt * 128:(dt + 1) * 128],
                                            s2[:, dt * 1024 + tc * 512: dt * 1024 + tc * 512 + 512],
                                            start=(dt == 0), stop=(dt == NDT - 1)),
                                        "tS" if dt == NDT - 1 else None, 1)
                                for dt in range(NDT):
                                    umm = sch.op(
                                        "tensor",
                                        lambda e, gpar=gpar, wpar=wpar, dt=dt, tc=tc: e.matmul(
                                            psUu[gpar][:],
                                            wub[wpar][:, dt * 128:(dt + 1) * 128],
                                            s2[:, dt * 1024 + tc * 512: dt * 1024 + tc * 512 + 512],
                                            start=(dt == 0), stop=(dt == NDT - 1)),
                                        "tS" if dt == NDT - 1 else None, 1)
                                gu_w_free[wpar] = umm
                                sch.wait("scalar", "tS", gmm)
                                sch.wait("scalar", "vS", sgf_free[gpar])
                                slv = sch.op(
                                    "scalar",
                                    lambda e, gpar=gpar: e.activation(
                                        sgf[gpar][:], psGg[gpar][:], AF.Silu),
                                    "sS", 1)
                                psGg_free[gpar] = slv
                                sch.wait("vector", "tS", umm)
                                sch.wait("vector", "sS", slv)
                                if ft == 0 and tc == 0:
                                    sch.wait("vector", "tS", act_free[ei])
                                if e_ < 32:
                                    sch.op(
                                        "vector",
                                        lambda e, gpar=gpar: e.tensor_tensor(
                                            prf[:], sgf[gpar][:], psUu[gpar][:], ALU.mult),
                                        None)
                                    if ft == 0 and tc == 0:
                                        sch.wait("vector", "vS", cmbbc_ready[epar])
                                    av = sch.op(
                                        "vector",
                                        lambda e, ei=ei, ft=ft, tc=tc, epar=epar: e.tensor_tensor(
                                            act_sb[ei][:, ft * 1024 + tc * 512: ft * 1024 + tc * 512 + 512],
                                            prf[:], cmbbc[epar][:, tc * 512: tc * 512 + 512], ALU.mult),
                                        "vS", 1)
                                else:
                                    av = sch.op(
                                        "vector",
                                        lambda e, ei=ei, ft=ft, tc=tc, gpar=gpar: e.tensor_tensor(
                                            act_sb[ei][:, ft * 1024 + tc * 512: ft * 1024 + tc * 512 + 512],
                                            sgf[gpar][:], psUu[gpar][:], ALU.mult),
                                        "vS", 1)
                                psUu_free[gpar] = av
                                sgf_free[gpar] = av
                                guv += 1
                        if e_ < 32:
                            cmbbc_free[epar] = av
                    acts_ready = sch.cnt["vS"]
                    ng = len(grp)
                    for dt in range(NDT):
                        dpar = dt % 2
                        sch.wait("gpsimd", "tS", wdb_free[dpar])
                        wdv = 0
                        for ei, e_ in enumerate(grp):
                            wdv = sch.op(
                                "gpsimd",
                                lambda e, e_=e_, dt=dt, ei=ei, dpar=dpar: e.dma_start(
                                    wdb[dpar][:, ei * 768:(ei + 1) * 768]
                                    .rearrange("p (ft c) -> p ft c", c=128),
                                    wd_slice(e_, dt)),
                                "dG", 16)
                        for tc in range(2):
                            ppar = tc
                            sch.wait("tensor", "dG", wdv)
                            sch.wait("tensor", "vS", max(acts_ready, psD_free[ppar]))
                            nmm = ng * 6
                            im = 0
                            for ei in range(ng):
                                for ft in range(6):
                                    dmm = sch.op(
                                        "tensor",
                                        lambda e, ppar=ppar, dpar=dpar, ei=ei, ft=ft, tc=tc,
                                        st_=(im == 0), sp_=(im == nmm - 1): e.matmul(
                                            psD[ppar][:],
                                            wdb[dpar][:, (ei * 6 + ft) * 128:(ei * 6 + ft) * 128 + 128],
                                            act_sb[ei][:, ft * 1024 + tc * 512: ft * 1024 + tc * 512 + 512],
                                            start=st_, stop=sp_),
                                        "tS" if im == nmm - 1 else None, 1)
                                    im += 1
                            sch.wait("vector", "tS", dmm)
                            moe_done_v = sch.op(
                                "vector",
                                lambda e, dt=dt, tc=tc, ppar=ppar: e.tensor_tensor(
                                    resid[:, dt * 1024 + tc * 512: dt * 1024 + tc * 512 + 512],
                                    resid[:, dt * 1024 + tc * 512: dt * 1024 + tc * 512 + 512],
                                    psD[ppar][:], ALU.add),
                                "vS", 1)
                            psD_free[ppar] = moe_done_v
                        wdb_free[dpar] = dmm
                    for ei in range(ng):
                        act_free[ei] = sch.cnt["tS"]

        sch.barrier()

        if stage == 3:
            sch.op("sync", lambda e: e.dma_start(dbg_s2[:], s2[:]), "dS", 16)
            sch.op("sync", lambda e: e.dma_start(dbg_res[:], resid[:]), "dS", 16)
            sch.op("sync", lambda e: e.dma_start(dbg_act[:, 0:6144], dbg_act_src[0][:]), "dS", 16)
            sch.op("sync", lambda e: e.dma_start(dbg_act[:, 6144:12288], dbg_act_src[1][:]), "dS", 16)
            sch.wait("sync", "dS", sch.cnt["dS"])
            return _emit(nc, sch, es)

        _ln(nc, sch, resid, None, ones_cb, ones_rf, moe_done_v, False, pfx="ln2")

        with ExitStack() as p4:
            sb4 = lambda name, shape, dt_: p4.enter_context(nc.sbuf_tensor(name, shape, dt_))
            osb = [sb4(f"osb{i}", [128, NDT * 128], F32) for i in range(2)]
            psX = [p4.enter_context(nc.psum_tensor(f"psX{i}", [128, 128], F32)) for i in range(2)]
            psX_free = [0, 0]
            osb_free = [0, 0]
            st_last = 0
            for tt in range(8):
                opar = tt % 2
                sch.wait("vector", "dS", osb_free[opar])
                for dt in range(NDT):
                    xpar = (tt * NDT + dt) % 2
                    sch.wait("tensor", "vS", psX_free[xpar])
                    sch.wait("tensor", "dS", v_id)
                    tr = sch.op(
                        "tensor",
                        lambda e, dt=dt, tt=tt, xpar=xpar: e.transpose(
                            psX[xpar][:],
                            resid[:, dt * 1024 + tt * 128: dt * 1024 + tt * 128 + 128],
                            ident[:]),
                        "tS", 1)
                    sch.wait("vector", "tS", tr)
                    cv = sch.op(
                        "vector",
                        lambda e, dt=dt, opar=opar, xpar=xpar: e.tensor_copy(
                            osb[opar][:, dt * 128:(dt + 1) * 128], psX[xpar][:]),
                        "vS", 1)
                    psX_free[xpar] = cv
                sch.wait("sync", "vS", cv)
                st_last = sch.op(
                    "sync",
                    lambda e, tt=tt, opar=opar: e.dma_start(
                        y[tt * 128:(tt + 1) * 128, :], osb[opar][:]),
                    "dS", 16)
                osb_free[opar] = st_last
            sch.wait("sync", "dS", st_last)

        return _emit(nc, sch, es)


def _emit(nc, sch, es):
    sems = {}
    for name in ("dG", "dS", "tS", "vS", "sS", "ph"):
        sems[name] = es.enter_context(nc.semaphore(name))

    block = es.enter_context(nc.Block())

    def run(engine_name):
        def f(eng):
            for step in sch.steps[engine_name]:
                if step[0] == "wait":
                    eng.wait_ge(sems[step[1]], step[2])
                else:
                    _, fn, inc, n = step
                    ins = fn(eng)
                    if inc is not None:
                        sem = sems["ph"] if inc in ("ph", "phx") else sems[inc]
                        ins.then_inc(sem, n if inc not in ("ph", "phx") else 1)
        return f

    block.tensor(run("tensor"))
    block.vector(run("vector"))
    block.scalar(run("scalar"))
    block.sync(run("sync"))
    block.gpsimd(run("gpsimd"))
    return nc



def make_core_inputs(ci, x, w_qkv, b_qkv, w_o, b_o, gate_w, wg_, wu_, wd_, sg_, su_, sd_):
    i = ci
    xs = [x[0, : i * 512], x[1, : (7 - i) * 512],
          x[0, i * 512:(i + 1) * 512], x[1, (7 - i) * 512:(8 - i) * 512]]
    xkv = np.concatenate(xs, axis=0)
    xkvT = np.ascontiguousarray(xkv.T)

    pos = np.concatenate([
        np.arange(i * 512), np.arange((7 - i) * 512),
        i * 512 + np.arange(512), (7 - i) * 512 + np.arange(512)]).astype(np.float64)
    inv = 1.0 / (10000.0 ** (np.arange(0, HD, 2, dtype=np.float64) / HD))
    invf = np.concatenate([inv, inv])
    emb = pos[None, :] * invf[:, None]
    ropeA_ = np.sin(emb).astype(np.float32)
    c = np.cos(emb)
    c[0:64, :] *= -1.0
    ropeC_ = c.astype(np.float32)

    batch_of_tile = np.zeros(NKT, dtype=np.int64)
    batch_of_tile[4 * i:28] = 1
    batch_of_tile[32:36] = 1
    mA = np.where(batch_of_tile == 0, 0.0, NEG).astype(np.float32)
    mB = np.where(batch_of_tile == 1, 0.0, NEG).astype(np.float32)
    masks_ = np.zeros((128, 2 * NKT), dtype=np.float32)
    masks_[:, 0:NKT] = mA[None, :]
    masks_[:, NKT:] = mB[None, :]

    tri = np.zeros((128, 4 * 512), dtype=np.float32)
    p_ = np.arange(128)
    q_ = np.arange(512)
    for k in range(4):
        tri[:, k * 512:(k + 1) * 512] = np.where(
            (k * 128 + p_[:, None]) > q_[None, :], NEG, 0.0)

    bcols_ = np.zeros((128, 48), dtype=np.float32)
    for h in range(NH):
        bcols_[:, h] = b_qkv[h * 384: h * 384 + 128]
        bcols_[:, 16 + h] = b_qkv[h * 384 + 128: h * 384 + 256]
    for dt in range(NDT):
        bcols_[:, 32 + dt] = b_o[dt * 128:(dt + 1) * 128]
    bvrow_ = np.zeros((1, D), dtype=np.float32)
    for h in range(NH):
        bvrow_[0, h * 128:(h + 1) * 128] = b_qkv[h * 384 + 256: h * 384 + 384]

    return dict(
        xkvT=xkvT, ropeA=ropeA_, ropeC=ropeC_, masks=masks_, trid=tri,
        identd=np.eye(128, dtype=np.float32), bcols=bcols_, bvrow=bvrow_,
        wqkv=w_qkv, wo=w_o, gatew=gate_w, wg=wg_, wu=wu_, wd=wd_,
        sgw=sg_, suw=su_, sdw=sd_)


def kernel(L0_attn_x, L0_attn_w_qkv, L0_attn_b_qkv, L0_attn_w_o, L0_attn_b_o,
           L0_ffn_gate_w, L0_ffn_wg, L0_ffn_wu, L0_ffn_wd, L0_ffn_sg, L0_ffn_su,
           L0_ffn_sd):
    args = [np.asarray(a, dtype=np.float32) for a in (
        L0_attn_x, L0_attn_w_qkv, L0_attn_b_qkv, L0_attn_w_o, L0_attn_b_o,
        L0_ffn_gate_w, L0_ffn_wg, L0_ffn_wu, L0_ffn_wd, L0_ffn_sg, L0_ffn_su,
        L0_ffn_sd)]
    x = args[0]
    in_maps = [make_core_inputs(i, x, *args[1:]) for i in range(NCORE)]
    nc = build_nc(stage=4)
    res = run_bass_kernel_spmd(nc, in_maps, core_ids=list(range(NCORE)))
    out = np.empty((B, S, D), dtype=np.float32)
    for i in range(NCORE):
        yi = res.results[i]["y"]
        out[0, i * 512:(i + 1) * 512] = yi[0:512]
        out[1, (7 - i) * 512:(8 - i) * 512] = yi[512:1024]
    return out
